# version 01
# baseline (speedup 1.0000x reference)
import numpy as np
import concourse.bass as bass
import concourse.mybir as mybir
from concourse.bass_utils import run_bass_kernel_spmd

F32 = mybir.dt.float32
BF16 = mybir.dt.bfloat16
AF = mybir.ActivationFunctionType
ALU = mybir.AluOpType
AX = mybir.AxisListType

NCORE = 8
B, S, D = 2, 4096, 2048
TPC = 1024
NT = TPC // 128
DC = D // 128
KBUF = 1792
EXPM = 20.0
STAGE = 5
GLA_CUT = 7
DBG = 0

ENGS = ("t", "v", "s", "y", "g")


def _prep_stationary(W, mb=128):
    K, M = W.shape
    Kc, Mb = K // 128, M // mb
    return np.ascontiguousarray(
        W.reshape(Kc, 128, Mb, mb).transpose(2, 1, 0, 3).reshape(Mb * 128, Kc * mb))


def _rope_tables(pos, dqk=128):
    inv = 1.0 / (10000.0 ** (np.arange(0, dqk, 2)[: dqk // 2].astype(np.float32) / dqk))
    ang = np.outer(pos.astype(np.float32), inv)
    return np.cos(ang).T.astype(np.float32), np.sin(ang).T.astype(np.float32)


def _perm_cols(W, blk=128):
    perm = np.concatenate([np.arange(0, blk, 2), np.arange(1, blk, 2)])
    idx = np.concatenate([h * blk + perm for h in range(W.shape[1] // blk)])
    return W[:, idx]


def _masks_for_core(core):
    r0 = (core % 4) * TPC
    out = np.empty((128, 8 * 896), np.float32)
    NEG = np.float32(-1e30)
    for qt in range(8):
        i = (r0 + qt * 128 + np.arange(128))[:, None]
        jf = np.arange(256)[None, :]
        front = np.where((jf < 256) & (jf <= i - 512), np.float32(0), NEG)
        jw = (r0 + qt * 128 - 512 + np.arange(640))[None, :]
        band = (jw <= i) & (jw > i - 512) & (jw >= 0)
        win = np.where(band, np.float32(0), NEG)
        out[:, qt * 896: qt * 896 + 256] = front
        out[:, qt * 896 + 256: (qt + 1) * 896] = win
    return out


def _host_prep(inp):
    import ml_dtypes
    bf = ml_dtypes.bfloat16
    f32 = np.float32
    xflat = np.ascontiguousarray(np.asarray(inp["L0_attn_a_x"], f32).reshape(B * S, D))

    sh = {}
    sh["wq"] = _prep_stationary(_perm_cols(np.asarray(inp["L0_attn_a_wq"], f32)))
    sh["wk"] = _prep_stationary(_perm_cols(np.asarray(inp["L0_attn_a_wk"], f32)))
    sh["wv"] = np.ascontiguousarray(np.asarray(inp["L0_attn_a_wv"], f32))
    sh["wo"] = np.ascontiguousarray(np.asarray(inp["L0_attn_a_wo"], f32))
    gw = np.asarray(inp["L0_ffn_gate_w"], f32)
    uw = np.asarray(inp["L0_ffn_up_w"], f32)
    dw = np.asarray(inp["L0_ffn_down_w"], f32)
    sh["gws"] = np.concatenate([_prep_stationary(gw[e].T) for e in range(8)], 0)
    sh["uws"] = np.concatenate([_prep_stationary(uw[e].T) for e in range(8)], 0)
    sh["dws"] = np.concatenate([np.ascontiguousarray(dw[e].T) for e in range(8)], 0)
    sh["sgs"] = _prep_stationary(np.asarray(inp["L0_ffn_sh_gate_w"], f32).T)
    sh["sus"] = _prep_stationary(np.asarray(inp["L0_ffn_sh_up_w"], f32).T)
    sh["sds"] = np.ascontiguousarray(np.asarray(inp["L0_ffn_sh_down_w"], f32).T)
    sh["gwr"] = np.ascontiguousarray(
        np.asarray(inp["L0_ffn_gate_weight"], f32).T.reshape(16, 128, 8)
        .transpose(1, 0, 2).reshape(128, 128))
    sh["bq"] = _prep_stationary(np.asarray(inp["L0_attn_b_Wq"], f32))
    sh["bk"] = _prep_stationary(np.asarray(inp["L0_attn_b_Wk"], f32))
    sh["bkg1"] = _prep_stationary(np.asarray(inp["L0_attn_b_Wkg1"], f32), mb=16)
    sh["bkg2"] = np.ascontiguousarray(np.asarray(inp["L0_attn_b_Wkg2"], f32))
    sh["bkg2b"] = np.ascontiguousarray(np.asarray(inp["L0_attn_b_bkg2"], f32).reshape(8, 128).T)
    sh["bv"] = np.ascontiguousarray(np.asarray(inp["L0_attn_b_Wv"], f32))
    sh["bgw"] = np.ascontiguousarray(np.asarray(inp["L0_attn_b_Wg"], f32))
    sh["bgb"] = np.ascontiguousarray(np.broadcast_to(np.asarray(inp["L0_attn_b_bg"], f32)[None, :], (128, D)))
    sh["bo"] = np.ascontiguousarray(np.asarray(inp["L0_attn_b_Wo"], f32))
    sh["ident"] = np.eye(128, dtype=f32).astype(bf)
    sh["identf"] = np.eye(128, dtype=f32)
    sh["triu"] = np.triu(np.ones((128, 128), f32)).astype(bf)
    sh["onesf"] = np.ones((1, 128), f32)
    sh["sel8"] = np.kron(np.eye(8, dtype=f32), np.ones((1, 128), f32))

    in_maps = []
    for c in range(NCORE):
        seq, r0, g0 = c // 4, (c % 4) * TPC, c * TPC
        m = dict(sh)
        m["x"] = np.ascontiguousarray(xflat[g0: g0 + TPC])
        m["xh"] = (np.zeros((512, D), f32) if r0 == 0
                   else np.ascontiguousarray(xflat[g0 - 512: g0]))
        m["xf"] = np.ascontiguousarray(xflat[seq * S: seq * S + 256])
        qc, qs = _rope_tables(r0 + np.arange(TPC))
        m["qcos"] = np.ascontiguousarray(qc * f32(1.0 / np.sqrt(128.0)))
        m["qsin"] = np.ascontiguousarray(qs * f32(1.0 / np.sqrt(128.0)))
        posk = np.concatenate([np.arange(256), np.maximum(r0 - 512 + np.arange(512), 0),
                               r0 + np.arange(TPC)])
        kc, ks = _rope_tables(posk)
        m["kcos"], m["ksin"] = np.ascontiguousarray(kc), np.ascontiguousarray(ks)
        m["masks"] = _masks_for_core(c).astype(bf)
        mv = np.zeros((128, 8), f32)
        for j in range(8):
            if j // 4 == c // 4 and j < c:
                mv[:, j] = 1.0
        m["mvec"] = mv
        m["mvi"] = np.float32(1.0) - mv
        in_maps.append(m)
    return in_maps


class Ctr:
    __slots__ = ("name", "v", "h")

    def __init__(self, name):
        self.name, self.v, self.h = name, 0, None

    def inc(self, n=1):
        self.v += n
        return self.v


def build_program():
    nc = bass.Bass("TRN2", target_bir_lowering=False, detect_race_conditions=False)

    def din(name, shape, dt=F32):
        return nc.dram_tensor(name, list(shape), dt, kind="ExternalInput").ap()

    DT = {}
    for name, shape, dt in [
        ("x", (TPC, D), F32), ("xh", (512, D), F32), ("xf", (256, D), F32),
        ("wq", (2048, 2048), F32), ("wk", (512, 2048), F32),
        ("wv", (2048, 512), F32), ("wo", (2048, 2048), F32),
        ("gws", (8 * 1408, 2048), F32), ("uws", (8 * 1408, 2048), F32),
        ("dws", (8 * 1408, 2048), F32),
        ("sgs", (2816, 2048), F32), ("sus", (2816, 2048), F32), ("sds", (2816, 2048), F32),
        ("gwr", (128, 128), F32),
        ("bq", (1024, 2048), F32), ("bk", (1024, 2048), F32),
        ("bkg1", (128, 256), F32), ("bkg2", (16, 1024), F32), ("bkg2b", (128, 8), F32),
        ("bv", (2048, 2048), F32), ("bgw", (2048, 2048), F32), ("bgb", (128, D), F32),
        ("bo", (2048, 2048), F32),
        ("ident", (128, 128), BF16), ("identf", (128, 128), F32),
        ("triu", (128, 128), BF16), ("onesf", (1, 128), F32), ("sel8", (8, 1024), F32),
        ("qcos", (64, TPC), F32), ("qsin", (64, TPC), F32),
        ("kcos", (64, KBUF), F32), ("ksin", (64, KBUF), F32),
        ("masks", (128, 8 * 896), BF16), ("mvec", (128, 8), F32), ("mvi", (128, 8), F32),
    ]:
        DT[name] = din(name, shape, dt)
    y_d = nc.dram_tensor("y", [TPC, D], F32, kind="ExternalOutput").ap()
    dbg_d = (nc.dram_tensor("dbg", [128, 61440], F32, kind="ExternalOutput").ap()
             if DBG else None)
    dbg_off = [0]
    cci_d = nc.dram_tensor("cci", [128, 8 * 257], F32).ap()
    cco_d = nc.dram_tensor("cco", [NCORE * 128, 8 * 257], F32).ap()

    import contextlib
    stack = contextlib.ExitStack()
    sems = {}
    for name in ("sq", "gq", "aq", "tm", "vm", "am", "gm", "ym", "wf", "cc"):
        sems[name] = Ctr(name)
        sems[name].h = stack.enter_context(nc.semaphore(name))
    sq, gq, aq = sems["sq"], sems["gq"], sems["aq"]
    tm, vm, am, gm, ym = sems["tm"], sems["vm"], sems["am"], sems["gm"], sems["ym"]
    wf, cc = sems["wf"], sems["cc"]
    MS = {"t": tm, "v": vm, "s": am, "y": ym, "g": gm}

    Q = {e: [] for e in ENGS}

    def em(eng, fn):
        Q[eng].append(fn)

    def wait(eng, ctr, val):
        assert isinstance(val, int), f"bad wait {ctr.name} {val!r}"
        if val <= 0:
            return
        em(eng, lambda e, s=ctr, v=val: e.wait_ge(s.h, v))

    LASTI = {e: True for e in ENGS}

    def ins(eng, make, inc=None, n=1):
        def f(e, mk=make, s=inc, nn=n):
            i = mk(e)
            if s is not None:
                i.then_inc(s.h, nn)
        em(eng, f)
        LASTI[eng] = inc is not None
        return inc.inc(n) if inc is not None else None

    def dump(buf_ap, cols):
        if not DBG:
            return
        o = dbg_off[0]
        em("g", lambda e, b=buf_ap, oo=o, c=cols:
           e.dma_start(out=dbg_d[:, oo: oo + c], in_=b).then_inc(gq.h, 16))
        gq.inc(16)
        dbg_off[0] += cols

    def barrier():
        assert LASTI["t"], "tensor stream must end phase on an inc'd instruction"
        if not LASTI["v"]:
            ins("v", lambda e: e.tensor_copy(out=TIN[:, 38:39], in_=TIN[:, 38:39]), vm)
        if not LASTI["s"]:
            ins("s", lambda e: e.activation(out=TIN[:, 39:40], in_=TIN[:, 38:39],
                                            func=AF.Copy), am)
        vals = [(tm, tm.v), (vm, vm.v), (am, am.v), (sq, sq.v), (gq, gq.v), (aq, aq.v)]
        for e in ENGS:
            for ctr, val in vals:
                if val > 0:
                    wait(e, ctr, val)

    X = stack.enter_context(nc.sbuf_tensor("X", [128, NT * 2048], F32))
    SCR = stack.enter_context(nc.sbuf_tensor("SCR", [128, 2048], F32))
    SCB = stack.enter_context(nc.sbuf_tensor("SCB", [128, 2048], BF16))
    TIN = stack.enter_context(nc.sbuf_tensor("TIN", [128, 40], F32))
    IDB = stack.enter_context(nc.sbuf_tensor("IDB", [128, 128], BF16))
    IDF = stack.enter_context(nc.sbuf_tensor("IDF", [128, 128], F32))
    TRI = stack.enter_context(nc.sbuf_tensor("TRI", [128, 128], BF16))
    ONEF = stack.enter_context(nc.sbuf_tensor("ONEF", [1, 128], F32))
    PSF = stack.enter_context(nc.psum_tensor("PSF", [128, 6 * 512], F32))
    PSB = stack.enter_context(nc.psum_tensor("PSB", [128, 2048], BF16))
    RNK0 = stack.enter_context(nc.sbuf_tensor("RNK0", [128, 8], F32))

    ring = {"buf": None, "nb": 0, "li": 0, "gi": 0, "free_at": []}

    def ring_begin(buf, nb):
        ring["buf"], ring["nb"], ring["li"] = buf, nb, 0

    def ring_load(src_ap, cols=2048):
        gi, li, nb = ring["gi"], ring["li"], ring["nb"]
        if li >= nb:
            assert len(ring["free_at"]) > gi - nb, "ring_free missing for older granule"
            ctr, val = ring["free_at"][gi - nb]
            wait("g", ctr, val)
        slot = li % nb
        buf = ring["buf"]
        em("g", lambda e, b=buf, sl=slot, src=src_ap, c=cols:
           e.dma_start(out=b[:, sl * 2048: sl * 2048 + c], in_=src).then_inc(gq.h, 16))
        tgt = gq.inc(16)
        ring["gi"] += 1
        ring["li"] += 1
        return slot, tgt

    def ring_free(ctr, val):
        ring["free_at"].append((ctr, val))

    def tc(i):
        return TIN[:, i:i + 1]

    def emit_ln_pair(srcs, dst_fm, dst_cs, tok0s, psb, scb2, stats=None):
        nA, nB = 0, 24
        sA, sB = srcs
        v_sq = ins("s", lambda e, s=sA: e.activation(
            out=SCR[:, 0:2048], in_=s, func=AF.Square, accum_out=tc(nA + 1)), am)
        v_sq = ins("s", lambda e, s=sB: e.activation(
            out=SCR[:, 0:2048], in_=s, func=AF.Square, accum_out=tc(nB + 1)), am)
        em("v", lambda e, s=sA: e.reduce_sum(out=tc(nA + 0), in_=s, axis=AX.X))
        em("v", lambda e, s=sB: e.reduce_sum(out=tc(nB + 0), in_=s, axis=AX.X))
        wait("v", am, v_sq)
        for n in (nA, nB):
            em("v", lambda e, n=n: e.tensor_scalar(out=tc(n + 2), in0=tc(n + 0),
               scalar1=1.0 / 2048, scalar2=None, op0=ALU.mult))
        for n in (nA, nB):
            em("v", lambda e, n=n: e.tensor_scalar(out=tc(n + 3), in0=tc(n + 1),
               scalar1=1.0 / 2048, scalar2=None, op0=ALU.mult))
        for n in (nA, nB):
            em("v", lambda e, n=n: e.tensor_tensor(out=tc(n + 4), in0=tc(n + 2),
               in1=tc(n + 2), op=ALU.mult))
        for n in (nA, nB):
            em("v", lambda e, n=n: e.tensor_tensor(out=tc(n + 5), in0=tc(n + 3),
               in1=tc(n + 4), op=ALU.subtract))
        for n in (nA, nB):
            em("v", lambda e, n=n: e.tensor_scalar(out=tc(n + 5), in0=tc(n + 5),
               scalar1=1e-5, scalar2=None, op0=ALU.add))
        v_rec = 0
        for n in (nA, nB):
            v_rec = ins("v", lambda e, n=n: e.reciprocal(out=tc(n + 6), in_=tc(n + 5)), vm)
        wait("s", vm, v_rec)
        v_rst = 0
        for n in (nA, nB):
            v_rst = ins("s", lambda e, n=n: e.activation(out=tc(n + 7), in_=tc(n + 6),
                                                         func=AF.Sqrt), am)
        wait("v", am, v_rst)
        for n in (nA, nB):
            em("v", lambda e, n=n: e.tensor_tensor(out=tc(18 + (n > 0)), in0=tc(n + 7),
               in1=tc(n + 7), op=ALU.mult))
        for n in (nA, nB):
            em("v", lambda e, n=n: e.tensor_tensor(out=tc(18 + (n > 0)), in0=tc(18 + (n > 0)),
               in1=tc(n + 5), op=ALU.mult))
        for n in (nA, nB):
            em("v", lambda e, n=n: e.tensor_scalar(out=tc(18 + (n > 0)), in0=tc(18 + (n > 0)),
               scalar1=-0.5, scalar2=1.5, op0=ALU.mult, op1=ALU.add))
        for n in (nA, nB):
            em("v", lambda e, n=n: e.tensor_tensor(out=tc(n + 6), in0=tc(n + 7),
               in1=tc(18 + (n > 0)), op=ALU.mult))
        if stats is not None:
            for hi, n in enumerate((nA, nB)):
                em("v", lambda e, n=n, d=stats[hi][0]: e.tensor_copy(out=d, in_=tc(n + 2)))
                em("v", lambda e, n=n, d=stats[hi][1]: e.tensor_copy(out=d, in_=tc(n + 6)))
        v_xn = {}
        v_xn[0] = ins("v", lambda e, s=sA: e.tensor_scalar(
            out=SCB[:, 0:2048], in0=s, scalar1=tc(nA + 2), scalar2=tc(nA + 6),
            op0=ALU.subtract, op1=ALU.mult), vm)
        v_xn[1] = ins("v", lambda e, s=sB: e.tensor_scalar(
            out=scb2[:, 0:2048], in0=s, scalar1=tc(nB + 2), scalar2=tc(nB + 6),
            op0=ALU.subtract, op1=ALU.mult), vm)
        ev = [0, 0]
        last = 0
        for half_i, (buf, tok0) in enumerate(((SCB, tok0s[0]), (scb2, tok0s[1]))):
            for g4 in range(4):
                half = (g4 % 2) * 1024
                wait("t", vm, v_xn[half_i])
                if ev[g4 % 2]:
                    wait("t", vm, ev[g4 % 2])
                v_tp = None
                for j in range(4):
                    dcc = g4 * 4 + j
                    v_tp = ins("t", lambda e, d=dcc, hh=half, jj=j, p=psb, b=buf: e.transpose(
                        p[:, hh + jj * 128: hh + (jj + 1) * 128],
                        b[:, d * 128: (d + 1) * 128], IDB[:]),
                        tm if j == 3 else None)
                wait("v", tm, v_tp)
                ap_out = dst_fm.ap().rearrange("p (c k) -> p c k", k=dst_cs)[
                    :, g4 * 4:(g4 + 1) * 4, tok0: tok0 + 128]
                ev[g4 % 2] = ins("v", lambda e, hh=half, o=ap_out, p=psb: e.tensor_copy(
                    out=o, in_=p[:, hh: hh + 512]), vm)
                last = ev[g4 % 2]
        return last

    for i in range(NT):
        ins("y", lambda e, i=i: e.dma_start(out=X[:, i * 2048:(i + 1) * 2048],
                                            in_=DT["x"][i * 128:(i + 1) * 128, :]), sq, 16)
    ins("y", lambda e: e.dma_start(out=IDB[:], in_=DT["ident"][:, :]), sq, 16)
    ins("y", lambda e: e.dma_start(out=IDF[:], in_=DT["identf"][:, :]), sq, 16)
    ins("y", lambda e: e.dma_start(out=TRI[:], in_=DT["triu"][:, :]), sq, 16)
    v_c0 = ins("y", lambda e: e.dma_start(out=ONEF[:], in_=DT["onesf"][:, :]), sq, 16)
    ins("v", lambda e: e.memset(TIN[:, 0:40], 0.0), vm)
    ins("v", lambda e: e.memset(TIN[:, 13:14], -EXPM))
    v_ms = ins("v", lambda e: e.memset(TIN[:, 14:15], float(-np.log(16.0))), vm)
    v_rk = ins("g", lambda e: e.iota(RNK0[:, 0:8], pattern=[[-1, 8]], base=7,
                                     channel_multiplier=0,
                                     allow_small_or_imprecise_dtypes=True), gm)
    for e in ENGS:
        wait(e, sq, v_c0)
        wait(e, vm, v_ms)
        wait(e, gm, v_rk)

    def sub_attn_a():
        with (
            nc.sbuf_tensor("xn_fm", [128, DC * 1024], BF16) as xn_fm,
            nc.sbuf_tensor("ring_a", [128, 3 * 2048], BF16) as ring_a,
            nc.sbuf_tensor("k_fm", [128, 4 * KBUF], BF16) as k_fm,
            nc.sbuf_tensor("v_tm", [128, 14 * 512], BF16) as v_tm,
        ):
            ring_begin(ring_a, 3)
            with (
                nc.sbuf_tensor("xnf_fm", [128, DC * 768], BF16) as xnf_fm,
                nc.sbuf_tensor("xht", [128, 2 * 2048], F32) as xht,
                nc.sbuf_tensor("ktab", [64, 2 * KBUF], F32) as ktab,
                nc.sbuf_tensor("scb2a", [128, 2048], BF16) as scb2a,
            ):
                psA, psb = PSF, PSB
                ins("y", lambda e: e.dma_start(out=ktab[:, 0:KBUF], in_=DT["kcos"][:, :]), sq, 16)
                ins("y", lambda e: e.dma_start(out=ktab[:, KBUF:], in_=DT["ksin"][:, :]), sq, 16)
                for i in range(0, NT, 2):
                    emit_ln_pair((X[:, i * 2048:(i + 1) * 2048],
                                  X[:, (i + 1) * 2048:(i + 2) * 2048]),
                                 xn_fm, 1024, (i * 128, (i + 1) * 128), psb, scb2a)
                lastln = 0
                for i in range(0, 6, 2):
                    srcs = []
                    for j in (i, i + 1):
                        srcs.append(DT["xf"][j * 128:(j + 1) * 128, :] if j < 2
                                    else DT["xh"][(j - 2) * 128:(j - 1) * 128, :])
                    if i >= 2:
                        wait("y", vm, lastln)
                    ins("y", lambda e, s=srcs[0]: e.dma_start(
                        out=xht[:, 0:2048], in_=s), sq, 16)
                    v_l = ins("y", lambda e, s=srcs[1]: e.dma_start(
                        out=xht[:, 2048:4096], in_=s), sq, 16)
                    wait("v", sq, v_l)
                    wait("s", sq, v_l)
                    lastln = emit_ln_pair((xht[:, 0:2048], xht[:, 2048:4096]),
                                          xnf_fm, 768, (i * 128, (i + 1) * 128), psb, scb2a)
                barrier()
                if STAGE == 2:
                    dump(xn_fm[:, 0:16384], 16384)
                    dump(xnf_fm[:, 0:12288], 12288)

                mvs = [("f", 768, 0, 256, 0, 0), ("f", 768, 256, 512, 512, 256),
                       ("o", 1024, 0, 512, 1024, 768), ("o", 1024, 512, 512, 1536, 1280)]
                for g in range(4):
                    slot, tgt = ring_load(DT["wk"][g * 128:(g + 1) * 128, :])
                    wait("t", gq, tgt)
                    v_mm = None
                    for kcc in range(16):
                        for (bn, cs, c0, w, p0, kb) in mvs:
                            bb = xnf_fm if bn == "f" else xn_fm
                            v_mm = ins("t", lambda e, sl=slot, k=kcc, b=bb, cs_=cs, c_=c0, w_=w, p_=p0:
                                       e.matmul(psA[:, p_: p_ + w_],
                                                ring_a[:, sl * 2048 + k * 128: sl * 2048 + (k + 1) * 128],
                                                b[:, k * cs_ + c_: k * cs_ + c_ + w_],
                                                start=(k == 0), stop=(k == 15)),
                                       tm if (kcc == 15 and p0 == 1536) else None, 1)
                    ring_free(tm, v_mm)
                    wait("v", tm, v_mm)
                    v_r = 0
                    for (bn, cs, c0, w, p0, kb) in mvs:
                        lo, hi = psA[0:64, p0: p0 + w], psA[64:128, p0: p0 + w]
                        co, si = ktab[0:64, kb: kb + w], ktab[0:64, KBUF + kb: KBUF + kb + w]
                        em("v", lambda e, l=lo, c=co, w_=w: e.tensor_tensor(
                            out=SCR[0:64, 0:w_], in0=l, in1=c, op=ALU.mult))
                        em("v", lambda e, h=hi, s2=si, w_=w: e.scalar_tensor_tensor(
                            out=SCR[0:64, 1024:1024 + w_], in0=h, scalar=-1.0, in1=s2,
                            op0=ALU.mult, op1=ALU.mult))
                        em("v", lambda e, g_=g, kb_=kb, w_=w: e.tensor_tensor(
                            out=k_fm[0:64, g_ * KBUF + kb_: g_ * KBUF + kb_ + w_],
                            in0=SCR[0:64, 0:w_], in1=SCR[0:64, 1024:1024 + w_], op=ALU.add))
                        em("v", lambda e, l=lo, s2=si, w_=w: e.tensor_tensor(
                            out=SCR[0:64, 0:w_], in0=l, in1=s2, op=ALU.mult))
                        em("v", lambda e, h=hi, c=co, w_=w: e.tensor_tensor(
                            out=SCR[0:64, 1024:1024 + w_], in0=h, in1=c, op=ALU.mult))
                        v_r = ins("v", lambda e, g_=g, kb_=kb, w_=w: e.tensor_tensor(
                            out=k_fm[64:128, g_ * KBUF + kb_: g_ * KBUF + kb_ + w_],
                            in0=SCR[0:64, 0:w_], in1=SCR[0:64, 1024:1024 + w_], op=ALU.add), vm)
                    wait("t", vm, v_r)

                sgroups = [list(range(0, 6)), list(range(6, 12)), list(range(12, 14))]
                for sgrp in sgroups:
                    nst = len(sgrp)
                    for kcc in range(16):
                        slot, tgt = ring_load(DT["wv"][kcc * 128:(kcc + 1) * 128, :], cols=512)
                        wait("t", gq, tgt)
                        for bi, st in enumerate(sgrp):
                            if st < 6:
                                lt = xnf_fm[:, kcc * 768 + st * 128: kcc * 768 + (st + 1) * 128]
                            else:
                                o = (st - 6) * 128
                                lt = xn_fm[:, kcc * 1024 + o: kcc * 1024 + o + 128]
                            v_mm = ins("t", lambda e, sl=slot, l=lt, b=bi, k=kcc:
                                e.matmul(psA[:, b * 512:(b + 1) * 512],
                                         l, ring_a[:, sl * 2048: sl * 2048 + 512],
                                         start=(k == 0), stop=(k == 15)),
                                tm if bi == nst - 1 else None, 1)
                        ring_free(tm, v_mm)
                    wait("v", tm, tm.v)
                    v_ev = 0
                    for bi, st in enumerate(sgrp):
                        v_ev = ins("v", lambda e, b=bi, st_=st: e.tensor_copy(
                            out=v_tm[:, st_ * 512:(st_ + 1) * 512],
                            in_=psA[:, b * 512:(b + 1) * 512]), vm)
                    wait("t", vm, v_ev)
                barrier()
                if STAGE == 2:
                    dump(k_fm[:, 0:4 * KBUF], 4 * KBUF)
                    dump(v_tm[:, 0:7168], 7168)

            with (
                nc.sbuf_tensor("q_grp", [128, 16 * 512], BF16) as q_grp,
                nc.sbuf_tensor("att_grp", [128, 16 * 512], BF16) as att_grp,
                nc.sbuf_tensor("p_nrm", [128, 2 * 896], BF16) as p_nrm,
                nc.sbuf_tensor("pT", [128, 2 * 896], BF16) as pT,
                nc.sbuf_tensor("mask_g", [128, 4 * 896], BF16) as mask_g,
            ):
                qtab = SCR[64:128, 0:1024]
                for grp in range(2):
                    t0 = grp * 512
                    if True:
                        psS, psb2 = PSF, PSB
                        v_mk = ins("y", lambda e, g_=grp: e.dma_start(
                            out=mask_g[:], in_=DT["masks"][:, g_ * 3584:(g_ + 1) * 3584]), sq, 16)
                        ins("y", lambda e, g_=grp: e.dma_start(
                            out=SCR[64:128, 0:512], in_=DT["qcos"][:, g_ * 512:(g_ + 1) * 512]), sq, 16)
                        v_qt = ins("y", lambda e, g_=grp: e.dma_start(
                            out=SCR[64:128, 512:1024], in_=DT["qsin"][:, g_ * 512:(g_ + 1) * 512]), sq, 16)
                        wait("v", sq, v_qt)
                        ropev = [0, 0]
                        for mh in range(16):
                            slot, tgt = ring_load(DT["wq"][mh * 128:(mh + 1) * 128, :])
                            wait("t", gq, tgt)
                            if mh >= 2:
                                wait("t", vm, ropev[mh % 2])
                            for kcc in range(16):
                                v_mm = ins("t", lambda e, sl=slot, k=kcc, m=mh, t0_=t0:
                                           e.matmul(psS[:, (m % 2) * 512:(m % 2) * 512 + 512],
                                                    ring_a[:, sl * 2048 + k * 128: sl * 2048 + (k + 1) * 128],
                                                    xn_fm[:, k * 1024 + t0_: k * 1024 + t0_ + 512],
                                                    start=(k == 0), stop=(k == 15)),
                                           tm if kcc == 15 else None, 1)
                            ring_free(tm, v_mm)
                            wait("v", tm, v_mm)
                            lo = psS[0:64, (mh % 2) * 512:(mh % 2) * 512 + 512]
                            hi = psS[64:128, (mh % 2) * 512:(mh % 2) * 512 + 512]
                            co, si = SCR[64:128, 0:512], SCR[64:128, 512:1024]
                            em("v", lambda e, l=lo, c=co: e.tensor_tensor(
                                out=SCR[0:64, 0:512], in0=l, in1=c, op=ALU.mult))
                            em("v", lambda e, h=hi, s2=si: e.scalar_tensor_tensor(
                                out=SCR[0:64, 1024:1536], in0=h, scalar=-1.0, in1=s2,
                                op0=ALU.mult, op1=ALU.mult))
                            em("v", lambda e, m=mh: e.tensor_tensor(
                                out=q_grp[0:64, m * 512:(m + 1) * 512],
                                in0=SCR[0:64, 0:512], in1=SCR[0:64, 1024:1536], op=ALU.add))
                            em("v", lambda e, l=lo, s2=si: e.tensor_tensor(
                                out=SCR[0:64, 0:512], in0=l, in1=s2, op=ALU.mult))
                            em("v", lambda e, h=hi, c=co: e.tensor_tensor(
                                out=SCR[0:64, 1024:1536], in0=h, in1=c, op=ALU.mult))
                            ropev[mh % 2] = ins("v", lambda e, m=mh: e.tensor_tensor(
                                out=q_grp[64:128, m * 512:(m + 1) * 512],
                                in0=SCR[0:64, 0:512], in1=SCR[0:64, 1024:1536], op=ALU.add), vm)
                        wait("t", vm, max(ropev))
                        wait("v", sq, v_mk)
                        for h in range(16):
                            g = h // 4
                            e2q = [0, 0, 0, 0]
                            pev = [0, 0]
                            avv = [0]
                            for ql in range(4):
                                qt = grp * 4 + ql
                                sc0 = (ql % 2) * 1024
                                if ql >= 2:
                                    wait("t", am, e2q[ql - 2])
                                for si_, (kc, w, p0) in enumerate(
                                        [(0, 256, 0), (256 + qt * 128, 256, 256),
                                         (512 + qt * 128, 384, 512)]):
                                    v_sc = ins("t", lambda e, h_=h, q_=ql, kc_=kc, w_=w,
                                               p0_=p0, s0=sc0, g_=g:
                                               e.matmul(psS[:, s0 + p0_: s0 + p0_ + w_],
                                                        q_grp[:, h_ * 512 + q_ * 128: h_ * 512 + (q_ + 1) * 128],
                                                        k_fm[:, g_ * KBUF + kc_: g_ * KBUF + kc_ + w_],
                                                        start=True, stop=True),
                                               tm if si_ == 2 else None, 1)
                                wait("v", tm, v_sc)
                                v_msk = ins("v", lambda e, q_=ql, s0=sc0: e.tensor_tensor(
                                    out=psS[:, s0: s0 + 896], in0=psS[:, s0: s0 + 896],
                                    in1=mask_g[:, q_ * 896:(q_ + 1) * 896], op=ALU.add), vm)
                                wait("s", vm, v_msk)
                                v_e1 = ins("s", lambda e, s0=sc0: e.activation(
                                    out=SCR[:, 0:896], in_=psS[:, s0: s0 + 896], func=AF.Exp,
                                    bias=tc(13), accum_out=tc(8)), am)
                                wait("v", am, v_e1)
                                v_l9 = ins("v", lambda e: e.tensor_scalar(
                                    out=tc(9), in0=tc(8), scalar1=1e-35, scalar2=None,
                                    op0=ALU.add), vm)
                                wait("s", vm, v_l9)
                                v_ln = ins("s", lambda e: e.activation(
                                    out=tc(10), in_=tc(9), func=AF.Ln), am)
                                wait("v", am, v_ln)
                                v_b2 = ins("v", lambda e: e.tensor_scalar(
                                    out=tc(11), in0=tc(10), scalar1=-1.0, scalar2=-EXPM,
                                    op0=ALU.mult, op1=ALU.add), vm)
                                wait("s", vm, v_b2)
                                e2q[ql] = ins("s", lambda e, q_=ql, s0=sc0: e.activation(
                                    out=p_nrm[:, (q_ % 2) * 896:(q_ % 2) * 896 + 896],
                                    in_=psS[:, s0: s0 + 896], func=AF.Exp, bias=tc(11)), am)
                                wait("t", am, e2q[ql])
                                if ql >= 2:
                                    wait("t", vm, pev[ql % 2])
                                v_tp = [0, 0]
                                for kt in range(7):
                                    hf = (kt % 2) * 1024
                                    if kt >= 2:
                                        wait("t", vm, v_tp[kt % 2])
                                    vt = ins("t", lambda e, q_=ql, kt_=kt, hh=hf: e.transpose(
                                        psb2[:, hh: hh + 128],
                                        p_nrm[:, (q_ % 2) * 896 + kt_ * 128: (q_ % 2) * 896 + (kt_ + 1) * 128],
                                        IDB[:]), tm)
                                    wait("v", tm, vt)
                                    v_tp[kt % 2] = ins("v", lambda e, q_=ql, kt_=kt, hh=hf:
                                                       e.tensor_copy(
                                        out=pT[:, (q_ % 2) * 896 + kt_ * 128: (q_ % 2) * 896 + (kt_ + 1) * 128],
                                        in_=psb2[:, hh: hh + 128]), vm)
                                pev[ql % 2] = max(v_tp)
                                wait("t", vm, pev[ql % 2])
                                if ql == 0:
                                    wait("t", vm, avv[0])
                                v_av = None
                                for kt in range(7):
                                    st = kt if kt < 2 else qt + kt
                                    v_av = ins("t", lambda e, q_=ql, kt_=kt, st_=st, g_=g:
                                               e.matmul(psS[:, 4 * 512 + q_ * 128: 4 * 512 + (q_ + 1) * 128],
                                                        v_tm[:, st_ * 512 + g_ * 128: st_ * 512 + (g_ + 1) * 128],
                                                        pT[:, (q_ % 2) * 896 + kt_ * 128: (q_ % 2) * 896 + (kt_ + 1) * 128],
                                                        start=(kt_ == 0), stop=(kt_ == 6)),
                                               tm if kt == 6 else None, 1)
                            wait("v", tm, v_av)
                            avv[0] = ins("v", lambda e, h_=h: e.tensor_copy(
                                out=att_grp[:, h_ * 512:(h_ + 1) * 512],
                                in_=psS[:, 4 * 512: 5 * 512]), vm)
                        wait("t", vm, avv[0])
                    barrier()
                    if STAGE == 2 and grp == 0:
                        dump(q_grp[:, 0:8192], 8192)
                        dump(att_grp[:, 0:8192], 8192)
                    for th in range(2):
                        for nh in range(2):
                            v_mm = None
                            for kcc in range(16):
                                slot, tgt = ring_load(DT["wo"][kcc * 128:(kcc + 1) * 128, :])
                                wait("t", gq, tgt)
                                for tt in range(2):
                                    for nn2 in range(2):
                                        bank = tt * 2 + nn2
                                        v_mm = ins("t", lambda e, sl=slot, k=kcc, t_=th * 2 + tt,
                                                   n_=nh * 2 + nn2, b=bank:
                                                   e.matmul(PSF[:, b * 512:(b + 1) * 512],
                                                            att_grp[:, k * 512 + t_ * 128: k * 512 + (t_ + 1) * 128],
                                                            ring_a[:, sl * 2048 + n_ * 512: sl * 2048 + (n_ + 1) * 512],
                                                            start=(k == 0), stop=(k == 15)),
                                                   tm if (tt == 1 and nn2 == 1) else None, 1)
                                ring_free(tm, v_mm)
                            wait("v", tm, v_mm)
                            v_xu = 0
                            for tt in range(2):
                                ti = grp * 4 + th * 2 + tt
                                for nn2 in range(2):
                                    b = tt * 2 + nn2
                                    n_ = nh * 2 + nn2
                                    v_xu = ins("v", lambda e, b_=b, n_=n_, ti_=ti:
                                               e.scalar_tensor_tensor(
                                        out=X[:, ti_ * 2048 + n_ * 512: ti_ * 2048 + (n_ + 1) * 512],
                                        in0=PSF[:, b_ * 512:(b_ + 1) * 512], scalar=1.0,
                                        in1=X[:, ti_ * 2048 + n_ * 512: ti_ * 2048 + (n_ + 1) * 512],
                                        op0=ALU.mult, op1=ALU.add), vm)
                            wait("t", vm, v_xu)
                    barrier()
        barrier()



    def sub_moe(app):
        tag = f"m{app}"
        dbgtgt = [0]
        with (
            nc.sbuf_tensor(f"hn{tag}", [128, DC * 1024], BF16) as hn_fm,
            nc.sbuf_tensor(f"ring{tag}", [128, 14 * 2048], BF16) as ring_m,
            nc.sbuf_tensor(f"an{tag}", [128, 11 * 1024], BF16) as a_n,
            nc.sbuf_tensor(f"wb{tag}", [128, 1024], F32) as wb,
            nc.sbuf_tensor(f"wT{tag}", [8, 1024], F32) as w_T,
            nc.sbuf_tensor(f"rt{tag}", [128, 8 * 40], F32) as RTS,
            nc.sbuf_tensor(f"gwrf{tag}", [128, 128], F32) as GWRF,
            nc.sbuf_tensor(f"sel{tag}", [8, 1024], F32) as SEL8,
            nc.sbuf_tensor(f"lnst{tag}", [128, 16], F32) as LNST,
            nc.sbuf_tensor(f"xcol{tag}", [128, 256], F32) as XCOL,
        ):
            ring_begin(ring_m, 14)
            ins("y", lambda e: e.dma_start(out=GWRF[:], in_=DT["gwr"][:, :]), sq, 16)
            v_gw = ins("y", lambda e: e.dma_start(out=SEL8[0:8, :], in_=DT["sel8"][:, :]), sq, 16)
            wait("t", sq, v_gw)
            wait("v", sq, v_gw)
            with nc.sbuf_tensor(f"scb2{tag}", [128, 2048], BF16) as scb2m:
                psb = PSB
                for i in range(0, NT, 2):
                    emit_ln_pair((X[:, i * 2048:(i + 1) * 2048],
                                  X[:, (i + 1) * 2048:(i + 2) * 2048]),
                                 hn_fm, 1024, (i * 128, (i + 1) * 128), psb, scb2m,
                                 stats=((LNST[:, 2 * i:2 * i + 1],
                                         LNST[:, 2 * i + 1:2 * i + 2]),
                                        (LNST[:, 2 * i + 2:2 * i + 3],
                                         LNST[:, 2 * i + 3:2 * i + 4])))
                barrier()
            if True:
                psR = PSF[:, 5 * 512: 6 * 512]
                psT = PSF[:, 0:1024]
                v_lg = None
                for t in range(8):
                    v_xc = ins("v", lambda e, t=t: e.tensor_scalar(
                        out=SCR[:, 0:2048], in0=X[:, t * 2048:(t + 1) * 2048],
                        scalar1=LNST[:, 2 * t:2 * t + 1], scalar2=None,
                        op0=ALU.subtract), vm)
                    wait("t", vm, v_xc)
                    evv = [0, 0]
                    for kc in range(16):
                        sl = kc % 2
                        if kc >= 2:
                            wait("t", vm, evv[sl])
                        v_tp = ins("t", lambda e, k=kc, sl=sl: e.transpose(
                            psT[:, sl * 512: sl * 512 + 128],
                            SCR[:, k * 128:(k + 1) * 128], IDF[:]), tm)
                        wait("v", tm, v_tp)
                        evv[sl] = ins("v", lambda e, sl=sl: e.tensor_copy(
                            out=XCOL[:, sl * 128:(sl + 1) * 128],
                            in_=psT[:, sl * 512: sl * 512 + 128]), vm)
                        wait("t", vm, evv[sl])
                        v_lg = ins("t", lambda e, t=t, k=kc, sl=sl: e.matmul(
                            psR[:, t * 8:(t + 1) * 8],
                            XCOL[:, sl * 128:(sl + 1) * 128],
                            GWRF[:, k * 8:(k + 1) * 8],
                            start=(k == 0), stop=(k == 15)),
                            tm if kc == 15 else None, 1)
                wait("v", tm, v_lg)
                wait("s", tm, v_lg)

                def rc(t, q):
                    return RTS[:, t * 40 + q: t * 40 + q + 1]

                def rr(t, a, bcount):
                    return RTS[:, t * 40 + a: t * 40 + a + bcount]

                for t in range(8):
                    em("v", lambda e, t=t: e.reduce_max(
                        out=rc(t, 0), in_=psR[:, t * 8:(t + 1) * 8], axis=AX.X,
                        negate=True))
                v_ng = None
                for t in range(8):
                    v_ng = ins("v", lambda e, t=t: e.tensor_scalar(
                        out=rc(t, 0), in0=rc(t, 0),
                        scalar1=LNST[:, 2 * t + 1:2 * t + 2], scalar2=None,
                        op0=ALU.mult), vm if t == 7 else None)
                wait("s", vm, v_ng)
                v_sc = None
                for t in range(8):
                    v_sc = ins("s", lambda e, t=t: e.activation(
                        out=rr(t, 4, 8), in_=psR[:, t * 8:(t + 1) * 8], func=AF.Exp,
                        scale=LNST[:, 2 * t + 1:2 * t + 2],
                        bias=rc(t, 0), accum_out=rc(t, 1)), am)
                wait("v", am, v_sc)
                for t in range(8):
                    em("v", lambda e, t=t: e.reciprocal(out=rc(t, 2), in_=rc(t, 1)))
                for t in range(8):
                    em("v", lambda e, t=t: e.tensor_scalar(
                        out=rr(t, 4, 8), in0=rr(t, 4, 8), scalar1=rc(t, 2),
                        scalar2=None, op0=ALU.mult))
                for t in range(8):
                    em("v", lambda e, t=t: e.tensor_tensor(
                        out=rr(t, 12, 4),
                        in0=RTS.ap().rearrange("p (b two) -> p b two", two=2)[
                            :, t * 20 + 2: t * 20 + 6, 0],
                        in1=RTS.ap().rearrange("p (b two) -> p b two", two=2)[
                            :, t * 20 + 2: t * 20 + 6, 1],
                        op=ALU.max))
                for t in range(8):
                    em("v", lambda e, t=t: e.tensor_copy(out=rr(t, 16, 4),
                                                         in_=RNK0[:, 4:8]))
                for o in range(1, 4):
                    for t in range(8):
                        em("v", lambda e, t=t, o=o: e.tensor_tensor(
                            out=rr(t, 36, 4 - o), in0=rr(t, 12, 4 - o),
                            in1=rr(t, 12 + o, 4 - o), op=ALU.is_ge))
                    for t in range(8):
                        em("v", lambda e, t=t, o=o: e.tensor_tensor(
                            out=rr(t, 16 + o, 4 - o), in0=rr(t, 16 + o, 4 - o),
                            in1=rr(t, 36, 4 - o), op=ALU.add))
                    for t in range(8):
                        em("v", lambda e, t=t, o=o: e.tensor_tensor(
                            out=rr(t, 16, 4 - o), in0=rr(t, 16, 4 - o),
                            in1=rr(t, 36, 4 - o), op=ALU.subtract))
                for t in range(8):
                    em("v", lambda e, t=t: e.tensor_scalar(
                        out=rr(t, 36, 4), in0=rr(t, 16, 4), scalar1=1.5,
                        scalar2=None, op0=ALU.is_le))
                for par in range(2):
                    for t in range(8):
                        em("v", lambda e, t=t, par=par: e.tensor_tensor(
                            out=RTS.ap().rearrange("p (b two) -> p b two", two=2)[
                                :, t * 20 + 10: t * 20 + 14, par],
                            in0=RTS.ap().rearrange("p (b two) -> p b two", two=2)[
                                :, t * 20 + 2: t * 20 + 6, par],
                            in1=rr(t, 36, 4), op=ALU.mult))
                for t in range(8):
                    em("v", lambda e, t=t: e.tensor_copy(out=rr(t, 28, 8),
                                                         in_=RNK0[:, 0:8]))
                for o in range(1, 8):
                    for t in range(8):
                        em("v", lambda e, t=t, o=o: e.tensor_tensor(
                            out=rr(t, 12, 8 - o), in0=rr(t, 20, 8 - o),
                            in1=rr(t, 20 + o, 8 - o), op=ALU.is_ge))
                    for t in range(8):
                        em("v", lambda e, t=t, o=o: e.tensor_tensor(
                            out=rr(t, 28 + o, 8 - o), in0=rr(t, 28 + o, 8 - o),
                            in1=rr(t, 12, 8 - o), op=ALU.add))
                    for t in range(8):
                        em("v", lambda e, t=t, o=o: e.tensor_tensor(
                            out=rr(t, 28, 8 - o), in0=rr(t, 28, 8 - o),
                            in1=rr(t, 12, 8 - o), op=ALU.subtract))
                for t in range(8):
                    em("v", lambda e, t=t: e.tensor_scalar(
                        out=rr(t, 12, 8), in0=rr(t, 28, 8), scalar1=1.5,
                        scalar2=None, op0=ALU.is_le))
                v_w = None
                for t in range(8):
                    v_w = ins("v", lambda e, t=t: e.tensor_tensor(
                        out=rr(t, 4, 8), in0=rr(t, 20, 8), in1=rr(t, 12, 8),
                        op=ALU.mult), vm)
                wait("t", vm, v_w)
                v_wt = [0, 0]
                for t in range(8):
                    if v_wt[t % 2]:
                        wait("t", vm, v_wt[t % 2])
                    v_tp = ins("t", lambda e, t=t: e.transpose(
                        psT[0:8, (t % 2) * 512: (t % 2) * 512 + 128],
                        rr(t, 4, 8), IDF[:]), tm)
                    wait("v", tm, v_tp)
                    v_wt[t % 2] = ins("v", lambda e, t=t: e.tensor_copy(
                        out=w_T[0:8, t * 128:(t + 1) * 128],
                        in_=psT[0:8, (t % 2) * 512: (t % 2) * 512 + 128]), vm)
                barrier()
                if STAGE == 3 and app == 1:
                    dump(hn_fm[:, 0:16384], 16384)
                    dump(RTS[:, 0:320], 320)
            if True:
                for ex in range(10):
                    routed = ex < 8
                    if routed:
                        gsrc, usrc, dsrc = DT["gws"], DT["uws"], DT["dws"]
                        fb0 = ex * 11
                    else:
                        gsrc, usrc, dsrc = DT["sgs"], DT["sus"], DT["sds"]
                        fb0 = (ex - 8) * 11
                    if routed:
                        v_bc = None
                        for nn2 in range(2):
                            v_bc = ins("t", lambda e, n=nn2, ex=ex: e.matmul(
                                PSF[:, (4 + n) * 512: (5 + n) * 512],
                                SEL8[0:8, ex * 128:(ex + 1) * 128],
                                w_T[0:8, n * 512:(n + 1) * 512],
                                start=True, stop=True), tm)
                        wait("v", tm, v_bc)
                        v_wb = None
                        for nn2 in range(2):
                            v_wb = ins("v", lambda e, n=nn2: e.tensor_copy(
                                out=wb[:, n * 512:(n + 1) * 512],
                                in_=PSF[:, (4 + n) * 512: (5 + n) * 512]),
                                vm)
                        wait("t", vm, v_wb)
                    aev = [0, 0]
                    for nh in range(2):
                        for fb in range(11):
                            bs = (fb % 2) * 2
                            if aev[fb % 2]:
                                wait("t", vm, aev[fb % 2])
                            v_gu = None
                            for gu, wsrc in ((0, gsrc), (1, usrc)):
                                slot, tgt = ring_load(wsrc[(fb0 + fb) * 128:(fb0 + fb + 1) * 128, :])
                                wait("t", gq, tgt)
                                for kc in range(16):
                                    v_gu = ins("t", lambda e, sl=slot, k=kc, n=nh, b=bs + gu:
                                               e.matmul(PSF[:, b * 512:(b + 1) * 512],
                                                        ring_m[:, sl * 2048 + k * 128: sl * 2048 + (k + 1) * 128],
                                                        hn_fm[:, k * 1024 + n * 512: k * 1024 + n * 512 + 512],
                                                        start=(k == 0), stop=(k == 15)),
                                               tm if kc == 15 else None, 1)
                                ring_free(tm, v_gu)
                            wait("s", tm, v_gu)
                            v_si = ins("s", lambda e, b=bs: e.activation(
                                out=SCR[:, 0:512],
                                in_=PSF[:, b * 512:(b + 1) * 512],
                                func=AF.Silu), am)
                            wait("v", am, v_si)
                            em("v", lambda e, b=bs: e.tensor_tensor(
                                out=SCR[:, 1024:1536],
                                in0=SCR[:, 0:512],
                                in1=PSF[:, (b + 1) * 512:(b + 2) * 512],
                                op=ALU.mult))
                            if routed:
                                aev[fb % 2] = ins("v", lambda e, n=nh, fb=fb: e.tensor_tensor(
                                    out=a_n[:, fb * 1024 + n * 512: fb * 1024 + (n + 1) * 512],
                                    in0=SCR[:, 1024:1536],
                                    in1=wb[:, n * 512:(n + 1) * 512], op=ALU.mult), vm)
                            else:
                                aev[fb % 2] = ins("v", lambda e, n=nh, fb=fb: e.tensor_copy(
                                    out=a_n[:, fb * 1024 + n * 512: fb * 1024 + (n + 1) * 512],
                                    in_=SCR[:, 1024:1536]), vm)
                    wait("t", vm, max(aev))
                    dslots = []
                    for k in range(11):
                        slot, tgt = ring_load(dsrc[(fb0 + k) * 128:(fb0 + k + 1) * 128, :])
                        dslots.append((slot, tgt))
                    xev = [0, 0]
                    dn_last = {}
                    for nn2 in range(4):
                        for t in range(8):
                            ci = nn2 * 8 + t
                            bank = ci % 2
                            if ci >= 2:
                                wait("t", vm, xev[bank])
                            v_dn = None
                            for k in range(11):
                                slot, tgt = dslots[k]
                                if ci == 0:
                                    wait("t", gq, tgt)
                                v_dn = ins("t", lambda e, sl=slot, k=k, n=nn2, t=t, b=bank:
                                           e.matmul(PSF[:, (4 + b) * 512:(5 + b) * 512],
                                                    a_n[:, k * 1024 + t * 128: k * 1024 + (t + 1) * 128],
                                                    ring_m[:, sl * 2048 + n * 512: sl * 2048 + (n + 1) * 512],
                                                    start=(k == 0), stop=(k == 10)),
                                           tm if k == 10 else None, 1)
                                if ci == 31:
                                    dn_last[k] = v_dn
                            wait("v", tm, v_dn)
                            xev[bank] = ins("v", lambda e, n=nn2, t=t, b=bank:
                                            e.scalar_tensor_tensor(
                                out=X[:, t * 2048 + n * 512: t * 2048 + (n + 1) * 512],
                                in0=PSF[:, (4 + b) * 512:(5 + b) * 512], scalar=1.0,
                                in1=X[:, t * 2048 + n * 512: t * 2048 + (n + 1) * 512],
                                op0=ALU.mult, op1=ALU.add), vm)
                    for k in range(11):
                        ring_free(tm, dn_last[10])
                    wait("t", vm, max(xev))
                    if STAGE == 3 and app == 1 and ex == 0:
                        wait("g", vm, max(aev))
                        dump(a_n[:, 0:11264], 11264)
                        dump(wb[:, 0:1024], 1024)
                        o = dbg_off[0]
                        em("g", lambda e, oo=o: e.dma_start(
                            out=dbg_d[0:8, oo: oo + 1024], in_=w_T[0:8, :]).then_inc(gq.h, 16))
                        gq.inc(16)
                        dbg_off[0] += 1024
                        dbgtgt[0] = gq.v
                    if STAGE == 3 and app == 1 and ex == 1 and dbgtgt[0]:
                        wait("v", gq, dbgtgt[0])
                barrier()



    LN16 = float(np.log(16.0))

    def sub_gla():
        import contextlib as _ctl
        gst = _ctl.ExitStack()
        xn3 = gst.enter_context(nc.sbuf_tensor("g_xn", [128, DC * 1024], BF16))
        ring_g = gst.enter_context(nc.sbuf_tensor("g_ring", [128, 2 * 2048], BF16))
        S = gst.enter_context(nc.sbuf_tensor("g_S", [128, 8 * 256], F32))
        SB16 = gst.enter_context(nc.sbuf_tensor("g_S16", [128, 8 * 256], BF16))
        DL = gst.enter_context(nc.sbuf_tensor("g_DL", [128, 64], F32))
        SUML = gst.enter_context(nc.sbuf_tensor("g_SUML", [128, 8], F32))
        z1b = gst.enter_context(nc.sbuf_tensor("g_z1b", [16, 1024], BF16))
        BKG1 = gst.enter_context(nc.sbuf_tensor("g_kg1", [128, 256], BF16))
        BKG2 = gst.enter_context(nc.sbuf_tensor("g_kg2", [16, 1024], BF16))
        KG2B = gst.enter_context(nc.sbuf_tensor("g_kg2b", [128, 8], F32))
        MV = gst.enter_context(nc.sbuf_tensor("g_mv", [128, 8], F32))
        MVI = gst.enter_context(nc.sbuf_tensor("g_mvi", [128, 8], F32))
        BGBB = gst.enter_context(nc.sbuf_tensor("g_bgb", [128, 2048], BF16))
        GZ = gst.enter_context(nc.sbuf_tensor("g_gz", [128, 128], F32))
        KDT = gst.enter_context(nc.sbuf_tensor("g_kdt", [128, 8 * 4 * 128], BF16))
        VT = gst.enter_context(nc.sbuf_tensor("g_vt", [128, 4 * 2048], BF16))
        if True:
            ring_begin(ring_g, 2)
            ins("g", lambda e: e.dma_start(out=BKG1[:], in_=DT["bkg1"][:, :]), gq, 16)
            ins("g", lambda e: e.dma_start(out=BKG2[0:16, :], in_=DT["bkg2"][:, :]), gq, 16)
            v_gg = ins("g", lambda e: e.dma_start(out=BGBB[:], in_=DT["bgb"][:, :]), gq, 16)
            ins("y", lambda e: e.dma_start(out=KG2B[:], in_=DT["bkg2b"][:, :]), sq, 16)
            ins("y", lambda e: e.dma_start(out=MV[:], in_=DT["mvec"][:, :]), sq, 16)
            v_mv = ins("y", lambda e: e.dma_start(out=MVI[:], in_=DT["mvi"][:, :]), sq, 16)
            ins("v", lambda e: e.memset(S[:], 0.0))
            ins("v", lambda e: e.memset(GZ[:], 0.0))
            v_ini = ins("v", lambda e: e.memset(SUML[:], 0.0), vm)
            with nc.sbuf_tensor("g_scb2", [128, 2048], BF16) as scb2g:
                psb = PSB
                for i in range(0, NT, 2):
                    emit_ln_pair((X[:, i * 2048:(i + 1) * 2048],
                                  X[:, (i + 1) * 2048:(i + 2) * 2048]),
                                 xn3, 1024, (i * 128, (i + 1) * 128), psb, scb2g)
                barrier()
            if True:
                ps0 = PSF
                v_z1 = None
                for n2 in range(2):
                    for kc in range(16):
                        v_z1 = ins("t", lambda e, k=kc, n=n2: e.matmul(
                            ps0[0:16, n * 512:(n + 1) * 512],
                            BKG1[:, k * 16:(k + 1) * 16],
                            xn3[:, k * 1024 + n * 512: k * 1024 + n * 512 + 512],
                            start=(k == 0), stop=(k == 15)),
                            tm if kc == 15 else None, 1)
                wait("v", tm, v_z1)
                v_z1b = ins("v", lambda e: e.tensor_copy(
                    out=z1b[0:16, :], in_=ps0[0:16, 0:1024]), vm)
                wait("t", vm, v_z1b)
            barrier()

            def emit_gk(h, half, psZ2, slot, guard_am, GKC, SCRB):
                v_z2 = ins("t", lambda e, h=h, half=half, sl=slot: e.matmul(
                    psZ2[:, sl * 512: sl * 512 + 512],
                    BKG2[0:16, h * 128:(h + 1) * 128],
                    z1b[0:16, half * 512: half * 512 + 512],
                    start=True, stop=True), tm)
                wait("s", tm, v_z2)
                ins("s", lambda e, h=h, sl=slot: e.activation(
                    out=SCR[:, 0:512], in_=psZ2[:, sl * 512: sl * 512 + 512],
                    func=AF.Sigmoid, bias=KG2B[:, h:h + 1]), am)
                v_ln = ins("s", lambda e: e.activation(
                    out=SCR[:, 512:1024], in_=SCR[:, 0:512], func=AF.Ln), am)
                wait("t", am, v_ln)
                wait("v", am, v_ln)
                if guard_am[slot]:
                    wait("v", am, guard_am[slot])
                v_sc = None
                for c in range(4):
                    v_sc = ins("v", lambda e, c=c, sl=slot: e.tensor_tensor_scan(
                        out=GKC[:, sl * 512 + c * 128: sl * 512 + (c + 1) * 128],
                        data0=SCR[:, 512 + c * 128: 512 + (c + 1) * 128],
                        data1=GZ[:, 0:128], initial=0.0,
                        op0=ALU.add, op1=ALU.add), vm)
                return v_sc

            def gkc_last(GKC, slot, c):
                return GKC[:, slot * 512 + c * 128 + 127: slot * 512 + c * 128 + 128]

            def emit_vproj(half, tag2):
                if True:
                    combos = [(s, n) for s in range(4) for n in range(4)]
                    for p0 in range(0, 16, 6):
                        sub = combos[p0:p0 + 6]
                        v_vm = None
                        for kc in range(16):
                            slot, tgt = ring_load(DT["bv"][kc * 128:(kc + 1) * 128, :])
                            wait("t", gq, tgt)
                            for bi, (s4, n4) in enumerate(sub):
                                v_vm = ins("t", lambda e, sl=slot, k=kc, s=s4, n=n4, b=bi, hf=half:
                                           e.matmul(PSF[:, b * 512:(b + 1) * 512],
                                                    xn3[:, k * 1024 + hf * 512 + s * 128: k * 1024 + hf * 512 + (s + 1) * 128],
                                                    ring_g[:, sl * 2048 + n * 512: sl * 2048 + (n + 1) * 512],
                                                    start=(k == 0), stop=(k == 15)),
                                           tm if bi == len(sub) - 1 else None, 1)
                            ring_free(tm, v_vm)
                        wait("v", tm, v_vm)
                        v_ve = None
                        for bi, (s4, n4) in enumerate(sub):
                            v_ve = ins("v", lambda e, s=s4, n=n4, b=bi: e.tensor_copy(
                                out=VT[:, s * 2048 + n * 512: s * 2048 + (n + 1) * 512],
                                in_=PSF[:, b * 512:(b + 1) * 512]), vm)
                        wait("t", vm, v_ve)

            wait("v", sq, v_mv)
            for half in range(2):
                with (
                    nc.sbuf_tensor(f"g_gkc1{half}", [128, 2 * 512], F32) as GKC,
                    nc.sbuf_tensor(f"g_scb1{half}", [128, 512], BF16) as SCRB,
                ):
                    psZ2 = PSF[:, 0:1024]
                    psK = PSF[:, 1024:2048]
                    psbT = PSB
                    guard = [0, 0]
                    kguard = [0, 0]
                    kdev = [0, 0]
                    for h in range(8):
                        sl = h % 2
                        v_sc = emit_gk(h, half, psZ2, sl, guard, GKC, SCRB)
                        slot, tgt = ring_load(DT["bk"][h * 128:(h + 1) * 128, :])
                        wait("t", gq, tgt)
                        if kguard[sl]:
                            wait("t", vm, kguard[sl])
                        v_k = None
                        for kc in range(16):
                            v_k = ins("t", lambda e, sll=slot, k=kc, hf=half, sl=sl: e.matmul(
                                psK[:, sl * 512: sl * 512 + 512],
                                ring_g[:, sll * 2048 + k * 128: sll * 2048 + (k + 1) * 128],
                                xn3[:, k * 1024 + hf * 512: k * 1024 + hf * 512 + 512],
                                start=(k == 0), stop=(k == 15)),
                                tm if kc == 15 else None, 1)
                        ring_free(tm, v_k)
                        wait("v", tm, v_k)
                        wait("v", vm, v_sc)
                        v_kd = None
                        v_ed = None
                        for c in range(4):
                            em("v", lambda e, lc=gkc_last(GKC, sl, c): e.tensor_scalar(
                                out=tc(20), in0=lc, scalar1=1.0 / 16, scalar2=-LN16,
                                op0=ALU.mult, op1=ALU.add))
                            em("v", lambda e, h=h, lc=gkc_last(GKC, sl, c): e.tensor_tensor(
                                out=SUML[:, h:h + 1], in0=SUML[:, h:h + 1], in1=lc,
                                op=ALU.add))
                            v_t21 = ins("v", lambda e, lc=gkc_last(GKC, sl, c): e.tensor_scalar(
                                out=tc(21), in0=lc, scalar1=1.0 / 16, scalar2=None,
                                op0=ALU.mult), vm)
                            wait("s", vm, v_t21)
                            v_ed = ins("s", lambda e, c=c, sl=sl, SCRB=SCRB, GKC=GKC: e.activation(
                                out=SCRB[:, c * 128:(c + 1) * 128],
                                in_=GKC[:, sl * 512 + c * 128: sl * 512 + (c + 1) * 128],
                                func=AF.Exp, scale=-1.0 / 16, bias=tc(20)), am)
                            ins("s", lambda e, h=h, hf=half, c=c: e.activation(
                                out=DL[:, h * 8 + hf * 4 + c: h * 8 + hf * 4 + c + 1],
                                in_=tc(21), func=AF.Exp), am)
                            wait("v", am, v_ed)
                            v_kd = ins("v", lambda e, c=c, sl=sl, psK=psK, SCRB=SCRB: e.tensor_tensor(
                                out=SCB[:, c * 128:(c + 1) * 128],
                                in0=psK[:, sl * 512 + c * 128: sl * 512 + (c + 1) * 128],
                                in1=SCRB[:, c * 128:(c + 1) * 128], op=ALU.mult), vm)
                        guard[sl] = am.v
                        kguard[sl] = v_kd
                        wait("t", vm, v_kd)
                        for c in range(4):
                            hfp = (c % 2) * 1024
                            if kdev[c % 2]:
                                wait("t", vm, kdev[c % 2])
                            v_tp = ins("t", lambda e, c=c, hfp=hfp: e.transpose(
                                psbT[:, hfp: hfp + 128], SCB[:, c * 128:(c + 1) * 128],
                                IDB[:]), tm)
                            wait("v", tm, v_tp)
                            kdev[c % 2] = ins("v", lambda e, h=h, c=c, hfp=hfp: e.tensor_copy(
                                out=KDT[:, (h * 4 + c) * 128:(h * 4 + c + 1) * 128],
                                in_=psbT[:, hfp: hfp + 128]), vm)
                        wait("t", vm, max(kdev))
                    barrier()
                emit_vproj(half, f"1{half}")
                barrier()
                if True:
                    psKV = PSF[:, 0:1024]
                    sev = [0, 0]
                    for c in range(4):
                        for h in range(8):
                            sl2 = h % 2
                            if sev[sl2]:
                                wait("t", vm, sev[sl2])
                            v_kv = ins("t", lambda e, h=h, c=c, sl2=sl2: e.matmul(
                                psKV[:, sl2 * 512: sl2 * 512 + 256],
                                KDT[:, (h * 4 + c) * 128:(h * 4 + c + 1) * 128],
                                VT[:, c * 2048 + h * 256: c * 2048 + (h + 1) * 256],
                                start=True, stop=True), tm)
                            wait("v", tm, v_kv)
                            sev[sl2] = ins("v", lambda e, h=h, hf=half, c=c, sl2=sl2:
                                           e.scalar_tensor_tensor(
                                out=S[:, h * 256:(h + 1) * 256],
                                in0=S[:, h * 256:(h + 1) * 256],
                                scalar=DL[:, h * 8 + hf * 4 + c: h * 8 + hf * 4 + c + 1],
                                in1=psKV[:, sl2 * 512: sl2 * 512 + 256],
                                op0=ALU.mult, op1=ALU.add), vm)
                    wait("t", vm, max(sev))
                barrier()

            if GLA_CUT >= 2:
                with nc.sbuf_tensor("g_exb", [128, 8 * 257], F32) as EXB:
                    for h in range(8):
                        em("v", lambda e, h=h: e.tensor_copy(
                            out=EXB[:, h * 257: h * 257 + 256],
                            in_=S[:, h * 256:(h + 1) * 256]))
                    v_ex = ins("v", lambda e: e.tensor_scalar(
                        out=TIN[:, 24:32], in0=SUML[:, 0:8], scalar1=1.0 / 16,
                        scalar2=None, op0=ALU.mult), vm)
                    wait("s", vm, v_ex)
                    v_ex2 = ins("s", lambda e: e.activation(
                        out=EXB.ap().rearrange("p (h q) -> p h q", q=257)[:, :, 256:257],
                        in_=TIN[:, 24:32], func=AF.Exp), am)
                    wait("y", vm, v_ex)
                    wait("y", am, v_ex2)
                    v_out = ins("y", lambda e: e.dma_start(out=cci_d[:, :], in_=EXB[:]), sq, 16)
                    wait("g", sq, v_out)
                    v_cc = ins("g", lambda e: e.collective_compute(
                        "AllGather", ALU.bypass,
                        replica_groups=[list(range(NCORE))],
                        ins=[cci_d.tensor.ap()], outs=[cco_d.tensor.ap()]), cc)
                    wait("v", cc, v_cc)
                    v_s0 = ins("v", lambda e: e.memset(S[:], 0.0), vm)

                with nc.sbuf_tensor("g_gin", [128, 2 * 2056], F32) as GIN:
                    cmb = [0, 0]
                    wait("y", cc, cc.v)
                    for j in range(NCORE):
                        gsl = j % 2
                        if j >= 2:
                            wait("y", vm, cmb[gsl])
                        v_gl = ins("y", lambda e, j=j, gsl=gsl: e.dma_start(
                            out=GIN[:, gsl * 2056:(gsl + 1) * 2056],
                            in_=cco_d[j * 128:(j + 1) * 128, :]), sq, 16)
                        wait("v", sq, v_gl)
                        if j == 0:
                            wait("v", vm, v_s0)
                        for h in range(8):
                            em("v", lambda e, j=j, h=h, gsl=gsl: e.tensor_scalar(
                                out=tc(24 + h), in0=GIN[:, gsl * 2056 + h * 257 + 256: gsl * 2056 + h * 257 + 257],
                                scalar1=MV[:, j:j + 1], scalar2=MVI[:, j:j + 1],
                                op0=ALU.mult, op1=ALU.add))
                        for h in range(8):
                            em("v", lambda e, h=h: e.tensor_scalar(
                                out=S[:, h * 256:(h + 1) * 256],
                                in0=S[:, h * 256:(h + 1) * 256],
                                scalar1=tc(24 + h), scalar2=None, op0=ALU.mult))
                            cmb[gsl] = ins("v", lambda e, j=j, h=h, gsl=gsl: e.scalar_tensor_tensor(
                                out=S[:, h * 256:(h + 1) * 256],
                                in0=GIN[:, gsl * 2056 + h * 257: gsl * 2056 + h * 257 + 256],
                                scalar=MV[:, j:j + 1],
                                in1=S[:, h * 256:(h + 1) * 256],
                                op0=ALU.mult, op1=ALU.add), vm)
                    v_sb = None
                    for h in range(8):
                        v_sb = ins("v", lambda e, h=h: e.tensor_copy(
                            out=SB16[:, h * 256:(h + 1) * 256],
                            in_=S[:, h * 256:(h + 1) * 256]), vm)
                    wait("t", vm, v_sb)
                barrier()
                if STAGE == 4:
                    dump(S[:, 0:2048], 2048)
                    dump(DL[:, 0:64], 64)
                    barrier()


            if GLA_CUT >= 3:
                for half in range(2):
                    with (
                        nc.sbuf_tensor(f"g_qd{half}", [128, 8 * 512], BF16) as QD,
                        nc.sbuf_tensor(f"g_kn{half}", [128, 8 * 512], BF16) as KN,
                        nc.sbuf_tensor(f"g_gh{half}", [128, 4 * 2048], BF16) as GH,
                    ):
                        with (
                            nc.sbuf_tensor(f"g_gkc2{half}", [128, 2 * 512], F32) as GKC,
                            nc.sbuf_tensor(f"g_scb2h{half}", [128, 512], BF16) as SCRB,
                        ):
                            psZ2 = PSF[:, 0:1024]
                            psK = PSF[:, 1024:2048]
                            psQ = PSF[:, 2048:3072]
                            psbT = PSB
                            guard = [0, 0]
                            kguard = [0, 0]
                            qguard = [0, 0]
                            kdev = [0, 0]
                            for h in range(8):
                                sl = h % 2
                                v_sc = emit_gk(h, half, psZ2, sl, guard, GKC, SCRB)
                                wait("s", vm, v_sc)
                                v_en = ins("s", lambda e, sl=sl, GKC=GKC: e.activation(
                                    out=SCR[:, 1024:1536], in_=GKC[:, sl * 512: sl * 512 + 512],
                                    func=AF.Exp, scale=-1.0 / 16, bias=tc(14)), am)
                                v_ep = ins("s", lambda e, sl=sl, GKC=GKC: e.activation(
                                    out=SCR[:, 1536:2048], in_=GKC[:, sl * 512: sl * 512 + 512],
                                    func=AF.Exp, scale=1.0 / 16), am)
                                slot, tgt = ring_load(DT["bk"][h * 128:(h + 1) * 128, :])
                                wait("t", gq, tgt)
                                if kguard[sl]:
                                    wait("t", vm, kguard[sl])
                                v_k = None
                                for kc in range(16):
                                    v_k = ins("t", lambda e, sll=slot, k=kc, hf=half, sl=sl: e.matmul(
                                        psK[:, sl * 512: sl * 512 + 512],
                                        ring_g[:, sll * 2048 + k * 128: sll * 2048 + (k + 1) * 128],
                                        xn3[:, k * 1024 + hf * 512: k * 1024 + hf * 512 + 512],
                                        start=(k == 0), stop=(k == 15)),
                                        tm if kc == 15 else None, 1)
                                ring_free(tm, v_k)
                                wait("v", tm, v_k)
                                wait("v", am, v_en)
                                v_kn = ins("v", lambda e, h=h, sl=sl, KN=KN, psK=psK: e.tensor_tensor(
                                    out=KN[:, h * 512:(h + 1) * 512],
                                    in0=psK[:, sl * 512: sl * 512 + 512],
                                    in1=SCR[:, 1024:1536], op=ALU.mult), vm)
                                v_kd = None
                                for c in range(4):
                                    v_b = ins("v", lambda e, lc=gkc_last(GKC, sl, c): e.tensor_scalar(
                                        out=tc(20), in0=lc, scalar1=1.0 / 16, scalar2=-LN16,
                                        op0=ALU.mult, op1=ALU.add), vm)
                                    wait("s", vm, v_b)
                                    v_ed = ins("s", lambda e, c=c, sl=sl, SCRB=SCRB, GKC=GKC: e.activation(
                                        out=SCRB[:, c * 128:(c + 1) * 128],
                                        in_=GKC[:, sl * 512 + c * 128: sl * 512 + (c + 1) * 128],
                                        func=AF.Exp, scale=-1.0 / 16, bias=tc(20)), am)
                                    wait("v", am, v_ed)
                                    v_kd = ins("v", lambda e, c=c, sl=sl, psK=psK, SCRB=SCRB: e.tensor_tensor(
                                        out=SCB[:, c * 128:(c + 1) * 128],
                                        in0=psK[:, sl * 512 + c * 128: sl * 512 + (c + 1) * 128],
                                        in1=SCRB[:, c * 128:(c + 1) * 128], op=ALU.mult), vm)
                                kguard[sl] = v_kd
                                wait("t", vm, v_kd)
                                for c in range(4):
                                    hfp = (c % 2) * 1024
                                    if kdev[c % 2]:
                                        wait("t", vm, kdev[c % 2])
                                    v_tp = ins("t", lambda e, c=c, hfp=hfp: e.transpose(
                                        psbT[:, hfp: hfp + 128], SCB[:, c * 128:(c + 1) * 128],
                                        IDB[:]), tm)
                                    wait("v", tm, v_tp)
                                    kdev[c % 2] = ins("v", lambda e, h=h, c=c, hfp=hfp: e.tensor_copy(
                                        out=KDT[:, (h * 4 + c) * 128:(h * 4 + c + 1) * 128],
                                        in_=psbT[:, hfp: hfp + 128]), vm)
                                wait("t", vm, max(kdev))
                                slot, tgt = ring_load(DT["bq"][h * 128:(h + 1) * 128, :])
                                wait("t", gq, tgt)
                                if qguard[sl]:
                                    wait("t", vm, qguard[sl])
                                v_q = None
                                for kc in range(16):
                                    v_q = ins("t", lambda e, sll=slot, k=kc, hf=half, sl=sl: e.matmul(
                                        psQ[:, sl * 512: sl * 512 + 512],
                                        ring_g[:, sll * 2048 + k * 128: sll * 2048 + (k + 1) * 128],
                                        xn3[:, k * 1024 + hf * 512: k * 1024 + hf * 512 + 512],
                                        start=(k == 0), stop=(k == 15)),
                                        tm if kc == 15 else None, 1)
                                ring_free(tm, v_q)
                                wait("v", tm, v_q)
                                wait("v", am, v_ep)
                                qguard[sl] = ins("v", lambda e, h=h, sl=sl, QD=QD, psQ=psQ: e.tensor_tensor(
                                    out=QD[:, h * 512:(h + 1) * 512],
                                    in0=psQ[:, sl * 512: sl * 512 + 512],
                                    in1=SCR[:, 1536:2048], op=ALU.mult), vm)
                                guard[sl] = am.v
                            wait("t", vm, max(qguard))
                            barrier()
                            if STAGE == 4 and half == 0:
                                dump(QD[:, 0:4096], 4096)
                                dump(KN[:, 0:4096], 4096)
                                dump(KDT[:, 0:4096], 4096)
                                barrier()
                        emit_vproj(half, f"2{half}")
                        barrier()
                        if STAGE == 4 and half == 0:
                            dump(VT[:, 0:8192], 8192)
                            barrier()
                        if True:
                            combos = [(s, n) for s in range(4) for n in range(4)]
                            for p0 in range(0, 16, 6):
                                sub = combos[p0:p0 + 6]
                                v_gm = None
                                for kc in range(16):
                                    slot, tgt = ring_load(DT["bgw"][kc * 128:(kc + 1) * 128, :])
                                    wait("t", gq, tgt)
                                    for bi, (s4, n4) in enumerate(sub):
                                        v_gm = ins("t", lambda e, sl=slot, k=kc, s=s4, n=n4, b=bi, hf=half:
                                                   e.matmul(PSF[:, b * 512:(b + 1) * 512],
                                                            xn3[:, k * 1024 + hf * 512 + s * 128: k * 1024 + hf * 512 + (s + 1) * 128],
                                                            ring_g[:, sl * 2048 + n * 512: sl * 2048 + (n + 1) * 512],
                                                            start=(k == 0), stop=(k == 15)),
                                                   tm if bi == len(sub) - 1 else None, 1)
                                    ring_free(tm, v_gm)
                                wait("v", tm, v_gm)
                                v_ge = None
                                for bi, (s4, n4) in enumerate(sub):
                                    v_ge = ins("v", lambda e, s=s4, n=n4, b=bi, GH=GH: e.scalar_tensor_tensor(
                                        out=GH[:, s * 2048 + n * 512: s * 2048 + (n + 1) * 512],
                                        in0=PSF[:, b * 512:(b + 1) * 512],
                                        scalar=1.0,
                                        in1=BGBB[:, n * 512:(n + 1) * 512],
                                        op0=ALU.mult, op1=ALU.add), vm)
                                wait("t", vm, v_ge)
                            barrier()
                            if STAGE == 4 and half == 0:
                                dump(GH[:, 0:8192], 8192)
                                barrier()
                        with (
                            nc.sbuf_tensor(f"g_asb{half}", [128, 2 * 128], BF16) as ASB,
                            nc.sbuf_tensor(f"g_on{half}", [128, 512], F32) as ON,
                            nc.sbuf_tensor(f"g_ogt{half}", [128, 2048], BF16) as OGT,
                            nc.sbuf_tensor(f"g_ogf{half}", [128, 2048], BF16) as OGF,
                        ):
                            psO = PSF[:, 0:1024]
                            psKV = PSF[:, 1024:2048]
                            psA = PSF[:, 2048:3072]
                            psbT = PSB
                            wvx = [0, 0, 0, 0]
                            if GLA_CUT >= 4:
                                ogfe = [0, 0]
                                for c in range(4):
                                    for n4 in range(4):
                                        if wvx[n4]:
                                            wait("t", vm, wvx[n4])
                                    v_og = None
                                    for hp in range(4):
                                        h0, h1 = 2 * hp, 2 * hp + 1
                                        for h in (h0, h1):
                                            sl = h % 2
                                            v_at = ins("t", lambda e, h=h, c=c, sl=sl, psA=psA, KN=KN, QD=QD: e.matmul(
                                                psA[:, sl * 512: sl * 512 + 128],
                                                KN[:, h * 512 + c * 128: h * 512 + (c + 1) * 128],
                                                QD[:, h * 512 + c * 128: h * 512 + (c + 1) * 128],
                                                start=True, stop=True), tm)
                                            wait("v", tm, v_at)
                                            v_as = ins("v", lambda e, h=h, sl=sl, ASB=ASB, psA=psA: e.tensor_tensor(
                                                out=ASB[:, sl * 128:(sl + 1) * 128],
                                                in0=psA[:, sl * 512: sl * 512 + 128],
                                                in1=TRI[:], op=ALU.mult), vm)
                                            wait("t", vm, v_as)
                                            ins("t", lambda e, h=h, c=c, sl=sl, psO=psO, QD=QD: e.matmul(
                                                psO[:, sl * 512: sl * 512 + 256],
                                                QD[:, h * 512 + c * 128: h * 512 + (c + 1) * 128],
                                                SB16[:, h * 256:(h + 1) * 256],
                                                start=True, stop=False))
                                            v_o = ins("t", lambda e, h=h, c=c, sl=sl, psO=psO, ASB=ASB: e.matmul(
                                                psO[:, sl * 512: sl * 512 + 256],
                                                ASB[:, sl * 128:(sl + 1) * 128],
                                                VT[:, c * 2048 + h * 256: c * 2048 + (h + 1) * 256],
                                                start=False, stop=True), tm)
                                            v_kv = ins("t", lambda e, h=h, c=c, sl=sl, psKV=psKV: e.matmul(
                                                psKV[:, sl * 512: sl * 512 + 256],
                                                KDT[:, (h * 4 + c) * 128:(h * 4 + c + 1) * 128],
                                                VT[:, c * 2048 + h * 256: c * 2048 + (h + 1) * 256],
                                                start=True, stop=True), tm)
                                        wait("v", tm, v_kv)
                                        nA, nB = 0, 24
                                        wait("s", tm, v_kv)
                                        v_sq2 = 0
                                        for hx, n in ((h0, nA), (h1, nB)):
                                            slx = hx % 2
                                            v_sq2 = ins("s", lambda e, n=n, slx=slx, psO=psO: e.activation(
                                                out=SCR[:, slx * 256: slx * 256 + 256],
                                                in_=psO[:, slx * 512: slx * 512 + 256],
                                                func=AF.Square, accum_out=tc(n + 1)), am)
                                        wait("v", am, v_sq2)
                                        for hx, n in ((h0, nA), (h1, nB)):
                                            slx = hx % 2
                                            em("v", lambda e, n=n, slx=slx, psO=psO: e.reduce_sum(
                                                out=tc(n + 0), in_=psO[:, slx * 512: slx * 512 + 256],
                                                axis=AX.X))
                                        for n in (nA, nB):
                                            em("v", lambda e, n=n: e.tensor_scalar(
                                                out=tc(n + 2), in0=tc(n + 0), scalar1=1.0 / 256,
                                                scalar2=None, op0=ALU.mult))
                                        for n in (nA, nB):
                                            em("v", lambda e, n=n: e.tensor_scalar(
                                                out=tc(n + 3), in0=tc(n + 1), scalar1=1.0 / 256,
                                                scalar2=None, op0=ALU.mult))
                                        for n in (nA, nB):
                                            em("v", lambda e, n=n: e.tensor_tensor(
                                                out=tc(n + 4), in0=tc(n + 2), in1=tc(n + 2), op=ALU.mult))
                                        for n in (nA, nB):
                                            em("v", lambda e, n=n: e.tensor_tensor(
                                                out=tc(n + 5), in0=tc(n + 3), in1=tc(n + 4), op=ALU.subtract))
                                        for n in (nA, nB):
                                            em("v", lambda e, n=n: e.tensor_scalar(
                                                out=tc(n + 5), in0=tc(n + 5), scalar1=1e-5,
                                                scalar2=None, op0=ALU.add))
                                        v_rc2 = 0
                                        for n in (nA, nB):
                                            v_rc2 = ins("v", lambda e, n=n: e.reciprocal(
                                                out=tc(n + 6), in_=tc(n + 5)), vm)
                                        wait("s", vm, v_rc2)
                                        v_rs2 = 0
                                        for n in (nA, nB):
                                            v_rs2 = ins("s", lambda e, n=n: e.activation(
                                                out=tc(n + 7), in_=tc(n + 6), func=AF.Sqrt), am)
                                        wait("v", am, v_rs2)
                                        for n in (nA, nB):
                                            em("v", lambda e, n=n: e.tensor_tensor(
                                                out=tc(18 + (n > 0)), in0=tc(n + 7), in1=tc(n + 7), op=ALU.mult))
                                        for n in (nA, nB):
                                            em("v", lambda e, n=n: e.tensor_tensor(
                                                out=tc(18 + (n > 0)), in0=tc(18 + (n > 0)), in1=tc(n + 5), op=ALU.mult))
                                        for n in (nA, nB):
                                            em("v", lambda e, n=n: e.tensor_scalar(
                                                out=tc(18 + (n > 0)), in0=tc(18 + (n > 0)),
                                                scalar1=-0.5, scalar2=1.5, op0=ALU.mult, op1=ALU.add))
                                        for n in (nA, nB):
                                            em("v", lambda e, n=n: e.tensor_tensor(
                                                out=tc(n + 6), in0=tc(n + 7), in1=tc(18 + (n > 0)), op=ALU.mult))
                                        for hx, n in ((h0, nA), (h1, nB)):
                                            slx = hx % 2
                                            em("v", lambda e, n=n, slx=slx, ON=ON, psO=psO: e.tensor_scalar(
                                                out=ON[:, slx * 256: slx * 256 + 256],
                                                in0=psO[:, slx * 512: slx * 512 + 256],
                                                scalar1=tc(n + 2), scalar2=tc(n + 6),
                                                op0=ALU.subtract, op1=ALU.mult))
                                        for hx in (h0, h1):
                                            slx = hx % 2
                                            em("v", lambda e, h=hx, hf=half, c=c, slx=slx, psKV=psKV:
                                               e.scalar_tensor_tensor(
                                                out=S[:, h * 256:(h + 1) * 256],
                                                in0=S[:, h * 256:(h + 1) * 256],
                                                scalar=DL[:, h * 8 + hf * 4 + c: h * 8 + hf * 4 + c + 1],
                                                in1=psKV[:, slx * 512: slx * 512 + 256],
                                                op0=ALU.mult, op1=ALU.add))
                                        v_sb2 = 0
                                        for hx in (h0, h1):
                                            v_sb2 = ins("v", lambda e, h=hx: e.tensor_copy(
                                                out=SB16[:, h * 256:(h + 1) * 256],
                                                in_=S[:, h * 256:(h + 1) * 256]), vm)
                                        wait("t", vm, v_sb2)
                                        wait("s", vm, v_sb2)
                                        v_gs = ins("s", lambda e, c=c, h0=h0, GH=GH: e.activation(
                                            out=SCR[:, 1024:1536],
                                            in_=GH[:, c * 2048 + h0 * 256: c * 2048 + h0 * 256 + 512],
                                            func=AF.Silu), am)
                                        wait("v", am, v_gs)
                                        v_og = ins("v", lambda e, h0=h0, OGT=OGT, ON=ON: e.tensor_tensor(
                                            out=OGT[:, h0 * 256: h0 * 256 + 512],
                                            in0=SCR[:, 1024:1536], in1=ON[:, 0:512],
                                            op=ALU.mult), vm)
                                    if GLA_CUT >= 7:
                                        wait("t", vm, v_og)
                                        for dcp in range(4):
                                            hfp = (dcp % 2) * 1024
                                            if ogfe[dcp % 2]:
                                                wait("t", vm, ogfe[dcp % 2])
                                            v_tp = None
                                            for j in range(4):
                                                dcc = dcp * 4 + j
                                                v_tp = ins("t", lambda e, d=dcc, hfp=hfp, j=j, psbT=psbT, OGT=OGT: e.transpose(
                                                    psbT[:, hfp + j * 128: hfp + (j + 1) * 128],
                                                    OGT[:, d * 128:(d + 1) * 128], IDB[:]),
                                                    tm if j == 3 else None)
                                            wait("v", tm, v_tp)
                                            ogfe[dcp % 2] = ins("v", lambda e, dcp=dcp, hfp=hfp, OGF=OGF, psbT=psbT: e.tensor_copy(
                                                out=OGF[:, dcp * 512:(dcp + 1) * 512],
                                                in_=psbT[:, hfp: hfp + 512]), vm)
                                        wait("t", vm, max(ogfe))
                                        v_wo = None
                                        for kc in range(16):
                                            slot, tgt = ring_load(DT["bo"][kc * 128:(kc + 1) * 128, :])
                                            wait("t", gq, tgt)
                                            if kc == 0:
                                                for n4 in range(4):
                                                    if wvx[n4]:
                                                        wait("t", vm, wvx[n4])
                                            for n4 in range(4):
                                                pso = psO if n4 < 2 else psKV
                                                v_wo = ins("t", lambda e, sl=slot, k=kc, n=n4, p=pso, OGF=OGF:
                                                           e.matmul(p[:, (n % 2) * 512:(n % 2) * 512 + 512],
                                                                    OGF[:, k * 128:(k + 1) * 128],
                                                                    ring_g[:, sl * 2048 + n * 512: sl * 2048 + (n + 1) * 512],
                                                                    start=(k == 0), stop=(k == 15)),
                                                           tm if n4 == 3 else None, 1)
                                            ring_free(tm, v_wo)
                                        wait("v", tm, v_wo)
                                        ti = half * 4 + c
                                        for n4 in range(4):
                                            pso = psO if n4 < 2 else psKV
                                            wvx[n4] = ins("v", lambda e, n=n4, ti=ti, p=pso:
                                                          e.scalar_tensor_tensor(
                                                out=X[:, ti * 2048 + n * 512: ti * 2048 + (n + 1) * 512],
                                                in0=p[:, (n % 2) * 512:(n % 2) * 512 + 512], scalar=1.0,
                                                in1=X[:, ti * 2048 + n * 512: ti * 2048 + (n + 1) * 512],
                                                op0=ALU.mult, op1=ALU.add), vm)

                        wait("t", vm, max(wvx))
                        barrier()
                        if STAGE == 4 and half == 0:
                            dump(X[:, 0:8192], 8192)
                            barrier()

        gst.close()
        barrier()

    if STAGE >= 2:
        sub_attn_a()
    if STAGE >= 3:
        sub_moe(1)
    if STAGE >= 4:
        sub_gla()
    if STAGE >= 5:
        sub_moe(2)

    v_fin = None
    for i in range(NT):
        v_fin = ins("s", lambda e, i=i: e.dma_start(
            out=y_d[i * 128:(i + 1) * 128, :], in_=X[:, i * 2048:(i + 1) * 2048]), aq, 16)
    wait("s", aq, v_fin)

    with nc.Block() as block:
        @block.tensor
        def _(e):
            for f in Q["t"]:
                f(e)

        @block.vector
        def _(e):
            for f in Q["v"]:
                f(e)

        @block.scalar
        def _(e):
            for f in Q["s"]:
                f(e)

        @block.sync
        def _(e):
            for f in Q["y"]:
                f(e)

        @block.gpsimd
        def _(e):
            for f in Q["g"]:
                f(e)

    stack.close()
    return nc


_NC_CACHE = [None]
_LAST_RES = [None]


def kernel(**inputs):
    in_maps = _host_prep(inputs)
    if _NC_CACHE[0] is None:
        _NC_CACHE[0] = build_program()
    nc = _NC_CACHE[0]
    res = run_bass_kernel_spmd(nc, in_maps, core_ids=list(range(NCORE)))
    _LAST_RES[0] = res
    ys = [np.asarray(res.results[c]["y"]) for c in range(NCORE)]
    return np.concatenate(ys, axis=0).reshape(B, S, D)


# version 04 (final)
# speedup vs baseline: 1.0393x; 1.0393x over previous
import numpy as np
import concourse.bass as bass
import concourse.mybir as mybir
from concourse.bass_utils import run_bass_kernel_spmd

F32 = mybir.dt.float32
BF16 = mybir.dt.bfloat16
AF = mybir.ActivationFunctionType
ALU = mybir.AluOpType
AX = mybir.AxisListType

NCORE = 8
B, S, D = 2, 4096, 2048
TPC = 1024
NT = TPC // 128
DC = D // 128
KBUF = 1792
EXPM = 20.0
STAGE = 5
GLA_CUT = 7
DBG = 0

ENGS = ("t", "v", "s", "y", "g")


def _prep_stationary(W, mb=128):
    K, M = W.shape
    Kc, Mb = K // 128, M // mb
    return np.ascontiguousarray(
        W.reshape(Kc, 128, Mb, mb).transpose(2, 1, 0, 3).reshape(Mb * 128, Kc * mb))


def _rope_tables(pos, dqk=128):
    inv = 1.0 / (10000.0 ** (np.arange(0, dqk, 2)[: dqk // 2].astype(np.float32) / dqk))
    ang = np.outer(pos.astype(np.float32), inv)
    return np.cos(ang).T.astype(np.float32), np.sin(ang).T.astype(np.float32)


def _perm_cols(W, blk=128):
    perm = np.concatenate([np.arange(0, blk, 2), np.arange(1, blk, 2)])
    idx = np.concatenate([h * blk + perm for h in range(W.shape[1] // blk)])
    return W[:, idx]


def _masks_for_core(core):
    r0 = (core % 4) * TPC
    out = np.empty((128, 8 * 896), np.float32)
    NEG = np.float32(-1e30)
    for qt in range(8):
        i = (r0 + qt * 128 + np.arange(128))[:, None]
        jf = np.arange(256)[None, :]
        front = np.where((jf < 256) & (jf <= i - 512), np.float32(0), NEG)
        jw = (r0 + qt * 128 - 512 + np.arange(640))[None, :]
        band = (jw <= i) & (jw > i - 512) & (jw >= 0)
        win = np.where(band, np.float32(0), NEG)
        out[:, qt * 896: qt * 896 + 256] = front
        out[:, qt * 896 + 256: (qt + 1) * 896] = win
    return out


def _host_prep(inp):
    import ml_dtypes
    bf = ml_dtypes.bfloat16
    f32 = np.float32
    xflat = np.ascontiguousarray(np.asarray(inp["L0_attn_a_x"], f32).reshape(B * S, D))

    sh = {}
    sh["wq"] = _prep_stationary(_perm_cols(np.asarray(inp["L0_attn_a_wq"], f32)))
    sh["wk"] = _prep_stationary(_perm_cols(np.asarray(inp["L0_attn_a_wk"], f32)))
    sh["wv"] = np.ascontiguousarray(np.asarray(inp["L0_attn_a_wv"], f32))
    sh["wo"] = np.ascontiguousarray(np.asarray(inp["L0_attn_a_wo"], f32))
    gw = np.asarray(inp["L0_ffn_gate_w"], f32)
    uw = np.asarray(inp["L0_ffn_up_w"], f32)
    dw = np.asarray(inp["L0_ffn_down_w"], f32)
    sh["gws"] = np.concatenate([_prep_stationary(gw[e].T) for e in range(8)], 0)
    sh["uws"] = np.concatenate([_prep_stationary(uw[e].T) for e in range(8)], 0)
    sh["dws"] = np.concatenate([np.ascontiguousarray(dw[e].T) for e in range(8)], 0)
    sh["sgs"] = _prep_stationary(np.asarray(inp["L0_ffn_sh_gate_w"], f32).T)
    sh["sus"] = _prep_stationary(np.asarray(inp["L0_ffn_sh_up_w"], f32).T)
    sh["sds"] = np.ascontiguousarray(np.asarray(inp["L0_ffn_sh_down_w"], f32).T)
    sh["gwr"] = np.ascontiguousarray(
        np.asarray(inp["L0_ffn_gate_weight"], f32).T.reshape(16, 128, 8)
        .transpose(1, 0, 2).reshape(128, 128))
    sh["bq"] = _prep_stationary(np.asarray(inp["L0_attn_b_Wq"], f32))
    sh["bk"] = _prep_stationary(np.asarray(inp["L0_attn_b_Wk"], f32))
    sh["bkg1"] = _prep_stationary(np.asarray(inp["L0_attn_b_Wkg1"], f32), mb=16)
    sh["bkg2"] = np.ascontiguousarray(np.asarray(inp["L0_attn_b_Wkg2"], f32))
    sh["bkg2b"] = np.ascontiguousarray(np.asarray(inp["L0_attn_b_bkg2"], f32).reshape(8, 128).T)
    sh["bv"] = np.ascontiguousarray(np.asarray(inp["L0_attn_b_Wv"], f32))
    sh["bgw"] = np.ascontiguousarray(np.asarray(inp["L0_attn_b_Wg"], f32))
    sh["bgb"] = np.ascontiguousarray(np.broadcast_to(np.asarray(inp["L0_attn_b_bg"], f32)[None, :], (128, D)))
    sh["bo"] = np.ascontiguousarray(np.asarray(inp["L0_attn_b_Wo"], f32))
    sh["ident"] = np.eye(128, dtype=f32).astype(bf)
    sh["identf"] = np.eye(128, dtype=f32)
    sh["triu"] = np.triu(np.ones((128, 128), f32)).astype(bf)
    sh["onesf"] = np.ones((1, 128), f32)
    sh["sel8"] = np.kron(np.eye(8, dtype=f32), np.ones((1, 128), f32))

    in_maps = []
    for c in range(NCORE):
        seq, r0, g0 = c // 4, (c % 4) * TPC, c * TPC
        m = dict(sh)
        m["x"] = np.ascontiguousarray(xflat[g0: g0 + TPC])
        m["xh"] = (np.zeros((512, D), f32) if r0 == 0
                   else np.ascontiguousarray(xflat[g0 - 512: g0]))
        m["xf"] = np.ascontiguousarray(xflat[seq * S: seq * S + 256])
        qc, qs = _rope_tables(r0 + np.arange(TPC))
        m["qcos"] = np.ascontiguousarray(qc * f32(1.0 / np.sqrt(128.0)))
        m["qsin"] = np.ascontiguousarray(qs * f32(1.0 / np.sqrt(128.0)))
        posk = np.concatenate([np.arange(256), np.maximum(r0 - 512 + np.arange(512), 0),
                               r0 + np.arange(TPC)])
        kc, ks = _rope_tables(posk)
        m["kcos"], m["ksin"] = np.ascontiguousarray(kc), np.ascontiguousarray(ks)
        m["masks"] = _masks_for_core(c).astype(bf)
        mv = np.zeros((128, 8), f32)
        for j in range(8):
            if j // 4 == c // 4 and j < c:
                mv[:, j] = 1.0
        m["mvec"] = mv
        m["mvi"] = np.float32(1.0) - mv
        in_maps.append(m)
    return in_maps


class Ctr:
    __slots__ = ("name", "v", "h")

    def __init__(self, name):
        self.name, self.v, self.h = name, 0, None

    def inc(self, n=1):
        self.v += n
        return self.v


def build_program():
    nc = bass.Bass("TRN2", target_bir_lowering=False, detect_race_conditions=False)

    def din(name, shape, dt=F32):
        return nc.dram_tensor(name, list(shape), dt, kind="ExternalInput").ap()

    DT = {}
    for name, shape, dt in [
        ("x", (TPC, D), F32), ("xh", (512, D), F32), ("xf", (256, D), F32),
        ("wq", (2048, 2048), F32), ("wk", (512, 2048), F32),
        ("wv", (2048, 512), F32), ("wo", (2048, 2048), F32),
        ("gws", (8 * 1408, 2048), F32), ("uws", (8 * 1408, 2048), F32),
        ("dws", (8 * 1408, 2048), F32),
        ("sgs", (2816, 2048), F32), ("sus", (2816, 2048), F32), ("sds", (2816, 2048), F32),
        ("gwr", (128, 128), F32),
        ("bq", (1024, 2048), F32), ("bk", (1024, 2048), F32),
        ("bkg1", (128, 256), F32), ("bkg2", (16, 1024), F32), ("bkg2b", (128, 8), F32),
        ("bv", (2048, 2048), F32), ("bgw", (2048, 2048), F32), ("bgb", (128, D), F32),
        ("bo", (2048, 2048), F32),
        ("ident", (128, 128), BF16), ("identf", (128, 128), F32),
        ("triu", (128, 128), BF16), ("onesf", (1, 128), F32), ("sel8", (8, 1024), F32),
        ("qcos", (64, TPC), F32), ("qsin", (64, TPC), F32),
        ("kcos", (64, KBUF), F32), ("ksin", (64, KBUF), F32),
        ("masks", (128, 8 * 896), BF16), ("mvec", (128, 8), F32), ("mvi", (128, 8), F32),
    ]:
        DT[name] = din(name, shape, dt)
    y_d = nc.dram_tensor("y", [TPC, D], F32, kind="ExternalOutput").ap()
    dbg_d = (nc.dram_tensor("dbg", [128, 61440], F32, kind="ExternalOutput").ap()
             if DBG else None)
    dbg_off = [0]
    cci_d = nc.dram_tensor("cci", [128, 8 * 257], F32).ap()
    cco_d = nc.dram_tensor("cco", [NCORE * 128, 8 * 257], F32).ap()

    import contextlib
    stack = contextlib.ExitStack()
    sems = {}
    for name in ("sq", "gq", "aq", "tm", "vm", "am", "gm", "ym", "wf", "cc"):
        sems[name] = Ctr(name)
        sems[name].h = stack.enter_context(nc.semaphore(name))
    sq, gq, aq = sems["sq"], sems["gq"], sems["aq"]
    tm, vm, am, gm, ym = sems["tm"], sems["vm"], sems["am"], sems["gm"], sems["ym"]
    wf, cc = sems["wf"], sems["cc"]
    MS = {"t": tm, "v": vm, "s": am, "y": ym, "g": gm}

    Q = {e: [] for e in ENGS}

    def em(eng, fn):
        Q[eng].append(fn)

    def wait(eng, ctr, val):
        assert isinstance(val, int), f"bad wait {ctr.name} {val!r}"
        if val <= 0:
            return
        em(eng, lambda e, s=ctr, v=val: e.wait_ge(s.h, v))

    LASTI = {e: True for e in ENGS}

    def ins(eng, make, inc=None, n=1):
        def f(e, mk=make, s=inc, nn=n):
            i = mk(e)
            if s is not None:
                i.then_inc(s.h, nn)
        em(eng, f)
        LASTI[eng] = inc is not None
        return inc.inc(n) if inc is not None else None

    def dump(buf_ap, cols):
        if not DBG:
            return
        o = dbg_off[0]
        em("g", lambda e, b=buf_ap, oo=o, c=cols:
           e.dma_start(out=dbg_d[:, oo: oo + c], in_=b).then_inc(gq.h, 16))
        gq.inc(16)
        dbg_off[0] += cols

    def barrier():
        assert LASTI["t"], "tensor stream must end phase on an inc'd instruction"
        if not LASTI["v"]:
            ins("v", lambda e: e.tensor_copy(out=TIN[:, 38:39], in_=TIN[:, 38:39]), vm)
        if not LASTI["s"]:
            ins("s", lambda e: e.activation(out=TIN[:, 39:40], in_=TIN[:, 38:39],
                                            func=AF.Copy), am)
        vals = [(tm, tm.v), (vm, vm.v), (am, am.v), (sq, sq.v), (gq, gq.v), (aq, aq.v)]
        for e in ENGS:
            for ctr, val in vals:
                if val > 0:
                    wait(e, ctr, val)

    X = stack.enter_context(nc.sbuf_tensor("X", [128, NT * 2048], F32))
    SCR = stack.enter_context(nc.sbuf_tensor("SCR", [128, 2048], F32))
    SCB = stack.enter_context(nc.sbuf_tensor("SCB", [128, 2048], BF16))
    TIN = stack.enter_context(nc.sbuf_tensor("TIN", [128, 40], F32))
    IDB = stack.enter_context(nc.sbuf_tensor("IDB", [128, 128], BF16))
    IDF = stack.enter_context(nc.sbuf_tensor("IDF", [128, 128], F32))
    TRI = stack.enter_context(nc.sbuf_tensor("TRI", [128, 128], BF16))
    ONEF = stack.enter_context(nc.sbuf_tensor("ONEF", [1, 128], F32))
    PSF = stack.enter_context(nc.psum_tensor("PSF", [128, 6 * 512], F32))
    PSB = stack.enter_context(nc.psum_tensor("PSB", [128, 2048], BF16))
    RNK0 = stack.enter_context(nc.sbuf_tensor("RNK0", [128, 8], F32))

    ring = {"buf": None, "nb": 0, "li": 0, "gi": 0, "free_at": []}

    def ring_begin(buf, nb):
        ring["buf"], ring["nb"], ring["li"] = buf, nb, 0

    def ring_load(src_ap, cols=2048):
        gi, li, nb = ring["gi"], ring["li"], ring["nb"]
        if li >= nb:
            assert len(ring["free_at"]) > gi - nb, "ring_free missing for older granule"
            ctr, val = ring["free_at"][gi - nb]
            wait("g", ctr, val)
        slot = li % nb
        buf = ring["buf"]
        em("g", lambda e, b=buf, sl=slot, src=src_ap, c=cols:
           e.dma_start(out=b[:, sl * 2048: sl * 2048 + c], in_=src).then_inc(gq.h, 16))
        tgt = gq.inc(16)
        ring["gi"] += 1
        ring["li"] += 1
        return slot, tgt

    def ring_free(ctr, val):
        ring["free_at"].append((ctr, val))

    def tc(i):
        return TIN[:, i:i + 1]

    def emit_ln_pair(srcs, dst_fm, dst_cs, tok0s, psb, scb2, stats=None):
        nA, nB = 0, 24
        sA, sB = srcs
        v_sq = ins("s", lambda e, s=sA: e.activation(
            out=SCR[:, 0:2048], in_=s, func=AF.Square, accum_out=tc(nA + 1)), am)
        v_sq = ins("s", lambda e, s=sB: e.activation(
            out=SCR[:, 0:2048], in_=s, func=AF.Square, accum_out=tc(nB + 1)), am)
        em("v", lambda e, s=sA: e.reduce_sum(out=tc(nA + 0), in_=s, axis=AX.X))
        em("v", lambda e, s=sB: e.reduce_sum(out=tc(nB + 0), in_=s, axis=AX.X))
        wait("v", am, v_sq)
        for n in (nA, nB):
            em("v", lambda e, n=n: e.tensor_scalar(out=tc(n + 2), in0=tc(n + 0),
               scalar1=1.0 / 2048, scalar2=None, op0=ALU.mult))
        for n in (nA, nB):
            em("v", lambda e, n=n: e.tensor_scalar(out=tc(n + 3), in0=tc(n + 1),
               scalar1=1.0 / 2048, scalar2=None, op0=ALU.mult))
        for n in (nA, nB):
            em("v", lambda e, n=n: e.tensor_tensor(out=tc(n + 4), in0=tc(n + 2),
               in1=tc(n + 2), op=ALU.mult))
        for n in (nA, nB):
            em("v", lambda e, n=n: e.tensor_tensor(out=tc(n + 5), in0=tc(n + 3),
               in1=tc(n + 4), op=ALU.subtract))
        for n in (nA, nB):
            em("v", lambda e, n=n: e.tensor_scalar(out=tc(n + 5), in0=tc(n + 5),
               scalar1=1e-5, scalar2=None, op0=ALU.add))
        v_rec = 0
        for n in (nA, nB):
            v_rec = ins("v", lambda e, n=n: e.reciprocal(out=tc(n + 6), in_=tc(n + 5)), vm)
        wait("s", vm, v_rec)
        v_rst = 0
        for n in (nA, nB):
            v_rst = ins("s", lambda e, n=n: e.activation(out=tc(n + 7), in_=tc(n + 6),
                                                         func=AF.Sqrt), am)
        wait("v", am, v_rst)
        for n in (nA, nB):
            em("v", lambda e, n=n: e.tensor_tensor(out=tc(18 + (n > 0)), in0=tc(n + 7),
               in1=tc(n + 7), op=ALU.mult))
        for n in (nA, nB):
            em("v", lambda e, n=n: e.tensor_tensor(out=tc(18 + (n > 0)), in0=tc(18 + (n > 0)),
               in1=tc(n + 5), op=ALU.mult))
        for n in (nA, nB):
            em("v", lambda e, n=n: e.tensor_scalar(out=tc(18 + (n > 0)), in0=tc(18 + (n > 0)),
               scalar1=-0.5, scalar2=1.5, op0=ALU.mult, op1=ALU.add))
        for n in (nA, nB):
            em("v", lambda e, n=n: e.tensor_tensor(out=tc(n + 6), in0=tc(n + 7),
               in1=tc(18 + (n > 0)), op=ALU.mult))
        if stats is not None:
            for hi, n in enumerate((nA, nB)):
                em("v", lambda e, n=n, d=stats[hi][0]: e.tensor_copy(out=d, in_=tc(n + 2)))
                em("v", lambda e, n=n, d=stats[hi][1]: e.tensor_copy(out=d, in_=tc(n + 6)))
        v_xn = {}
        v_xn[0] = ins("v", lambda e, s=sA: e.tensor_scalar(
            out=SCB[:, 0:2048], in0=s, scalar1=tc(nA + 2), scalar2=tc(nA + 6),
            op0=ALU.subtract, op1=ALU.mult), vm)
        v_xn[1] = ins("v", lambda e, s=sB: e.tensor_scalar(
            out=scb2[:, 0:2048], in0=s, scalar1=tc(nB + 2), scalar2=tc(nB + 6),
            op0=ALU.subtract, op1=ALU.mult), vm)
        ev = [0, 0]
        last = 0
        for half_i, (buf, tok0) in enumerate(((SCB, tok0s[0]), (scb2, tok0s[1]))):
            for g4 in range(4):
                half = (g4 % 2) * 1024
                wait("t", vm, v_xn[half_i])
                if ev[g4 % 2]:
                    wait("t", vm, ev[g4 % 2])
                v_tp = None
                for j in range(4):
                    dcc = g4 * 4 + j
                    v_tp = ins("t", lambda e, d=dcc, hh=half, jj=j, p=psb, b=buf: e.transpose(
                        p[:, hh + jj * 128: hh + (jj + 1) * 128],
                        b[:, d * 128: (d + 1) * 128], IDB[:]),
                        tm if j == 3 else None)
                wait("v", tm, v_tp)
                ap_out = dst_fm.ap().rearrange("p (c k) -> p c k", k=dst_cs)[
                    :, g4 * 4:(g4 + 1) * 4, tok0: tok0 + 128]
                ev[g4 % 2] = ins("v", lambda e, hh=half, o=ap_out, p=psb: e.tensor_copy(
                    out=o, in_=p[:, hh: hh + 512]), vm)
                last = ev[g4 % 2]
        return last

    for i in range(NT):
        ins("y", lambda e, i=i: e.dma_start(out=X[:, i * 2048:(i + 1) * 2048],
                                            in_=DT["x"][i * 128:(i + 1) * 128, :]), sq, 16)
    ins("y", lambda e: e.dma_start(out=IDB[:], in_=DT["ident"][:, :]), sq, 16)
    ins("y", lambda e: e.dma_start(out=IDF[:], in_=DT["identf"][:, :]), sq, 16)
    ins("y", lambda e: e.dma_start(out=TRI[:], in_=DT["triu"][:, :]), sq, 16)
    v_c0 = ins("y", lambda e: e.dma_start(out=ONEF[:], in_=DT["onesf"][:, :]), sq, 16)
    ins("v", lambda e: e.memset(TIN[:, 0:40], 0.0), vm)
    ins("v", lambda e: e.memset(TIN[:, 13:14], -EXPM))
    v_ms = ins("v", lambda e: e.memset(TIN[:, 14:15], float(-np.log(16.0))), vm)
    v_rk = ins("g", lambda e: e.iota(RNK0[:, 0:8], pattern=[[-1, 8]], base=7,
                                     channel_multiplier=0,
                                     allow_small_or_imprecise_dtypes=True), gm)
    for e in ENGS:
        wait(e, sq, v_c0)
        wait(e, vm, v_ms)
        wait(e, gm, v_rk)

    def sub_attn_a():
        with (
            nc.sbuf_tensor("xn_fm", [128, DC * 1024], BF16) as xn_fm,
            nc.sbuf_tensor("ring_a", [128, 3 * 2048], BF16) as ring_a,
            nc.sbuf_tensor("k_fm", [128, 4 * KBUF], BF16) as k_fm,
            nc.sbuf_tensor("v_tm", [128, 14 * 512], BF16) as v_tm,
        ):
            ring_begin(ring_a, 3)
            with (
                nc.sbuf_tensor("xnf_fm", [128, DC * 768], BF16) as xnf_fm,
                nc.sbuf_tensor("xht", [128, 2 * 2048], F32) as xht,
                nc.sbuf_tensor("ktab", [64, 2 * KBUF], F32) as ktab,
                nc.sbuf_tensor("scb2a", [128, 2048], BF16) as scb2a,
            ):
                psA, psb = PSF, PSB
                ins("y", lambda e: e.dma_start(out=ktab[:, 0:KBUF], in_=DT["kcos"][:, :]), sq, 16)
                ins("y", lambda e: e.dma_start(out=ktab[:, KBUF:], in_=DT["ksin"][:, :]), sq, 16)
                for i in range(0, NT, 2):
                    emit_ln_pair((X[:, i * 2048:(i + 1) * 2048],
                                  X[:, (i + 1) * 2048:(i + 2) * 2048]),
                                 xn_fm, 1024, (i * 128, (i + 1) * 128), psb, scb2a)
                lastln = 0
                for i in range(0, 6, 2):
                    srcs = []
                    for j in (i, i + 1):
                        srcs.append(DT["xf"][j * 128:(j + 1) * 128, :] if j < 2
                                    else DT["xh"][(j - 2) * 128:(j - 1) * 128, :])
                    if i >= 2:
                        wait("y", vm, lastln)
                    ins("y", lambda e, s=srcs[0]: e.dma_start(
                        out=xht[:, 0:2048], in_=s), sq, 16)
                    v_l = ins("y", lambda e, s=srcs[1]: e.dma_start(
                        out=xht[:, 2048:4096], in_=s), sq, 16)
                    wait("v", sq, v_l)
                    wait("s", sq, v_l)
                    lastln = emit_ln_pair((xht[:, 0:2048], xht[:, 2048:4096]),
                                          xnf_fm, 768, (i * 128, (i + 1) * 128), psb, scb2a)
                barrier()
                if STAGE == 2:
                    dump(xn_fm[:, 0:16384], 16384)
                    dump(xnf_fm[:, 0:12288], 12288)

                mvs = [("f", 768, 0, 256, 0, 0), ("f", 768, 256, 512, 512, 256),
                       ("o", 1024, 0, 512, 1024, 768), ("o", 1024, 512, 512, 1536, 1280)]
                for g in range(4):
                    slot, tgt = ring_load(DT["wk"][g * 128:(g + 1) * 128, :])
                    wait("t", gq, tgt)
                    v_mm = None
                    for kcc in range(16):
                        for (bn, cs, c0, w, p0, kb) in mvs:
                            bb = xnf_fm if bn == "f" else xn_fm
                            v_mm = ins("t", lambda e, sl=slot, k=kcc, b=bb, cs_=cs, c_=c0, w_=w, p_=p0:
                                       e.matmul(psA[:, p_: p_ + w_],
                                                ring_a[:, sl * 2048 + k * 128: sl * 2048 + (k + 1) * 128],
                                                b[:, k * cs_ + c_: k * cs_ + c_ + w_],
                                                start=(k == 0), stop=(k == 15)),
                                       tm if (kcc == 15 and p0 == 1536) else None, 1)
                    ring_free(tm, v_mm)
                    wait("v", tm, v_mm)
                    v_r = 0
                    for (bn, cs, c0, w, p0, kb) in mvs:
                        lo, hi = psA[0:64, p0: p0 + w], psA[64:128, p0: p0 + w]
                        co, si = ktab[0:64, kb: kb + w], ktab[0:64, KBUF + kb: KBUF + kb + w]
                        em("v", lambda e, l=lo, c=co, w_=w: e.tensor_tensor(
                            out=SCR[0:64, 0:w_], in0=l, in1=c, op=ALU.mult))
                        em("v", lambda e, h=hi, s2=si, w_=w: e.scalar_tensor_tensor(
                            out=SCR[0:64, 1024:1024 + w_], in0=h, scalar=-1.0, in1=s2,
                            op0=ALU.mult, op1=ALU.mult))
                        em("v", lambda e, g_=g, kb_=kb, w_=w: e.tensor_tensor(
                            out=k_fm[0:64, g_ * KBUF + kb_: g_ * KBUF + kb_ + w_],
                            in0=SCR[0:64, 0:w_], in1=SCR[0:64, 1024:1024 + w_], op=ALU.add))
                        em("v", lambda e, l=lo, s2=si, w_=w: e.tensor_tensor(
                            out=SCR[0:64, 0:w_], in0=l, in1=s2, op=ALU.mult))
                        em("v", lambda e, h=hi, c=co, w_=w: e.tensor_tensor(
                            out=SCR[0:64, 1024:1024 + w_], in0=h, in1=c, op=ALU.mult))
                        v_r = ins("v", lambda e, g_=g, kb_=kb, w_=w: e.tensor_tensor(
                            out=k_fm[64:128, g_ * KBUF + kb_: g_ * KBUF + kb_ + w_],
                            in0=SCR[0:64, 0:w_], in1=SCR[0:64, 1024:1024 + w_], op=ALU.add), vm)
                    wait("t", vm, v_r)

                sgroups = [list(range(0, 6)), list(range(6, 12)), list(range(12, 14))]
                for sgrp in sgroups:
                    nst = len(sgrp)
                    for kcc in range(16):
                        slot, tgt = ring_load(DT["wv"][kcc * 128:(kcc + 1) * 128, :], cols=512)
                        wait("t", gq, tgt)
                        for bi, st in enumerate(sgrp):
                            if st < 6:
                                lt = xnf_fm[:, kcc * 768 + st * 128: kcc * 768 + (st + 1) * 128]
                            else:
                                o = (st - 6) * 128
                                lt = xn_fm[:, kcc * 1024 + o: kcc * 1024 + o + 128]
                            v_mm = ins("t", lambda e, sl=slot, l=lt, b=bi, k=kcc:
                                e.matmul(psA[:, b * 512:(b + 1) * 512],
                                         l, ring_a[:, sl * 2048: sl * 2048 + 512],
                                         start=(k == 0), stop=(k == 15)),
                                tm if bi == nst - 1 else None, 1)
                        ring_free(tm, v_mm)
                    wait("v", tm, tm.v)
                    v_ev = 0
                    for bi, st in enumerate(sgrp):
                        v_ev = ins("v", lambda e, b=bi, st_=st: e.tensor_copy(
                            out=v_tm[:, st_ * 512:(st_ + 1) * 512],
                            in_=psA[:, b * 512:(b + 1) * 512]), vm)
                    wait("t", vm, v_ev)
                barrier()
                if STAGE == 2:
                    dump(k_fm[:, 0:4 * KBUF], 4 * KBUF)
                    dump(v_tm[:, 0:7168], 7168)

            with (
                nc.sbuf_tensor("q_grp", [128, 16 * 512], BF16) as q_grp,
                nc.sbuf_tensor("att_grp", [128, 16 * 512], BF16) as att_grp,
                nc.sbuf_tensor("p_nrm", [128, 2 * 896], BF16) as p_nrm,
                nc.sbuf_tensor("pT", [128, 2 * 896], BF16) as pT,
                nc.sbuf_tensor("mask_g", [128, 4 * 896], BF16) as mask_g,
            ):
                qtab = SCR[64:128, 0:1024]
                for grp in range(2):
                    t0 = grp * 512
                    if True:
                        psS, psb2 = PSF, PSB
                        v_mk = ins("y", lambda e, g_=grp: e.dma_start(
                            out=mask_g[:], in_=DT["masks"][:, g_ * 3584:(g_ + 1) * 3584]), sq, 16)
                        ins("y", lambda e, g_=grp: e.dma_start(
                            out=SCR[64:128, 0:512], in_=DT["qcos"][:, g_ * 512:(g_ + 1) * 512]), sq, 16)
                        v_qt = ins("y", lambda e, g_=grp: e.dma_start(
                            out=SCR[64:128, 512:1024], in_=DT["qsin"][:, g_ * 512:(g_ + 1) * 512]), sq, 16)
                        wait("v", sq, v_qt)
                        ropev = [0, 0]
                        for mh in range(16):
                            slot, tgt = ring_load(DT["wq"][mh * 128:(mh + 1) * 128, :])
                            wait("t", gq, tgt)
                            if mh >= 2:
                                wait("t", vm, ropev[mh % 2])
                            for kcc in range(16):
                                v_mm = ins("t", lambda e, sl=slot, k=kcc, m=mh, t0_=t0:
                                           e.matmul(psS[:, (m % 2) * 512:(m % 2) * 512 + 512],
                                                    ring_a[:, sl * 2048 + k * 128: sl * 2048 + (k + 1) * 128],
                                                    xn_fm[:, k * 1024 + t0_: k * 1024 + t0_ + 512],
                                                    start=(k == 0), stop=(k == 15)),
                                           tm if kcc == 15 else None, 1)
                            ring_free(tm, v_mm)
                            wait("v", tm, v_mm)
                            lo = psS[0:64, (mh % 2) * 512:(mh % 2) * 512 + 512]
                            hi = psS[64:128, (mh % 2) * 512:(mh % 2) * 512 + 512]
                            co, si = SCR[64:128, 0:512], SCR[64:128, 512:1024]
                            em("v", lambda e, l=lo, c=co: e.tensor_tensor(
                                out=SCR[0:64, 0:512], in0=l, in1=c, op=ALU.mult))
                            em("v", lambda e, h=hi, s2=si: e.scalar_tensor_tensor(
                                out=SCR[0:64, 1024:1536], in0=h, scalar=-1.0, in1=s2,
                                op0=ALU.mult, op1=ALU.mult))
                            em("v", lambda e, m=mh: e.tensor_tensor(
                                out=q_grp[0:64, m * 512:(m + 1) * 512],
                                in0=SCR[0:64, 0:512], in1=SCR[0:64, 1024:1536], op=ALU.add))
                            em("v", lambda e, l=lo, s2=si: e.tensor_tensor(
                                out=SCR[0:64, 0:512], in0=l, in1=s2, op=ALU.mult))
                            em("v", lambda e, h=hi, c=co: e.tensor_tensor(
                                out=SCR[0:64, 1024:1536], in0=h, in1=c, op=ALU.mult))
                            ropev[mh % 2] = ins("v", lambda e, m=mh: e.tensor_tensor(
                                out=q_grp[64:128, m * 512:(m + 1) * 512],
                                in0=SCR[0:64, 0:512], in1=SCR[0:64, 1024:1536], op=ALU.add), vm)
                        wait("t", vm, max(ropev))
                        wait("v", sq, v_mk)
                        for h in range(16):
                            g = h // 4
                            e2q = [0, 0, 0, 0]
                            pev = [0, 0]
                            avv = [0]
                            for ql in range(4):
                                qt = grp * 4 + ql
                                sc0 = (ql % 2) * 1024
                                if ql >= 2:
                                    wait("t", am, e2q[ql - 2])
                                for si_, (kc, w, p0) in enumerate(
                                        [(0, 256, 0), (256 + qt * 128, 256, 256),
                                         (512 + qt * 128, 384, 512)]):
                                    v_sc = ins("t", lambda e, h_=h, q_=ql, kc_=kc, w_=w,
                                               p0_=p0, s0=sc0, g_=g:
                                               e.matmul(psS[:, s0 + p0_: s0 + p0_ + w_],
                                                        q_grp[:, h_ * 512 + q_ * 128: h_ * 512 + (q_ + 1) * 128],
                                                        k_fm[:, g_ * KBUF + kc_: g_ * KBUF + kc_ + w_],
                                                        start=True, stop=True),
                                               tm if si_ == 2 else None, 1)
                                wait("v", tm, v_sc)
                                v_msk = ins("v", lambda e, q_=ql, s0=sc0: e.tensor_tensor(
                                    out=psS[:, s0: s0 + 896], in0=psS[:, s0: s0 + 896],
                                    in1=mask_g[:, q_ * 896:(q_ + 1) * 896], op=ALU.add), vm)
                                wait("s", vm, v_msk)
                                v_e1 = ins("s", lambda e, s0=sc0: e.activation(
                                    out=SCR[:, 0:896], in_=psS[:, s0: s0 + 896], func=AF.Exp,
                                    bias=tc(13), accum_out=tc(8)), am)
                                wait("v", am, v_e1)
                                v_l9 = ins("v", lambda e: e.tensor_scalar(
                                    out=tc(9), in0=tc(8), scalar1=1e-35, scalar2=None,
                                    op0=ALU.add), vm)
                                wait("s", vm, v_l9)
                                v_ln = ins("s", lambda e: e.activation(
                                    out=tc(10), in_=tc(9), func=AF.Ln), am)
                                wait("v", am, v_ln)
                                v_b2 = ins("v", lambda e: e.tensor_scalar(
                                    out=tc(11), in0=tc(10), scalar1=-1.0, scalar2=-EXPM,
                                    op0=ALU.mult, op1=ALU.add), vm)
                                wait("s", vm, v_b2)
                                e2q[ql] = ins("s", lambda e, q_=ql, s0=sc0: e.activation(
                                    out=p_nrm[:, (q_ % 2) * 896:(q_ % 2) * 896 + 896],
                                    in_=psS[:, s0: s0 + 896], func=AF.Exp, bias=tc(11)), am)
                                wait("t", am, e2q[ql])
                                if ql >= 2:
                                    wait("t", vm, pev[ql % 2])
                                v_tp = [0, 0]
                                for kt in range(7):
                                    hf = (kt % 2) * 1024
                                    if kt >= 2:
                                        wait("t", vm, v_tp[kt % 2])
                                    vt = ins("t", lambda e, q_=ql, kt_=kt, hh=hf: e.transpose(
                                        psb2[:, hh: hh + 128],
                                        p_nrm[:, (q_ % 2) * 896 + kt_ * 128: (q_ % 2) * 896 + (kt_ + 1) * 128],
                                        IDB[:]), tm)
                                    wait("v", tm, vt)
                                    v_tp[kt % 2] = ins("v", lambda e, q_=ql, kt_=kt, hh=hf:
                                                       e.tensor_copy(
                                        out=pT[:, (q_ % 2) * 896 + kt_ * 128: (q_ % 2) * 896 + (kt_ + 1) * 128],
                                        in_=psb2[:, hh: hh + 128]), vm)
                                pev[ql % 2] = max(v_tp)
                                wait("t", vm, pev[ql % 2])
                                if ql == 0:
                                    wait("t", vm, avv[0])
                                v_av = None
                                for kt in range(7):
                                    st = kt if kt < 2 else qt + kt
                                    v_av = ins("t", lambda e, q_=ql, kt_=kt, st_=st, g_=g:
                                               e.matmul(psS[:, 4 * 512 + q_ * 128: 4 * 512 + (q_ + 1) * 128],
                                                        v_tm[:, st_ * 512 + g_ * 128: st_ * 512 + (g_ + 1) * 128],
                                                        pT[:, (q_ % 2) * 896 + kt_ * 128: (q_ % 2) * 896 + (kt_ + 1) * 128],
                                                        start=(kt_ == 0), stop=(kt_ == 6)),
                                               tm if kt == 6 else None, 1)
                            wait("v", tm, v_av)
                            avv[0] = ins("v", lambda e, h_=h: e.tensor_copy(
                                out=att_grp[:, h_ * 512:(h_ + 1) * 512],
                                in_=psS[:, 4 * 512: 5 * 512]), vm)
                        wait("t", vm, avv[0])
                    barrier()
                    if STAGE == 2 and grp == 0:
                        dump(q_grp[:, 0:8192], 8192)
                        dump(att_grp[:, 0:8192], 8192)
                    for th in range(2):
                        for nh in range(2):
                            v_mm = None
                            for kcc in range(16):
                                slot, tgt = ring_load(DT["wo"][kcc * 128:(kcc + 1) * 128, :])
                                wait("t", gq, tgt)
                                for tt in range(2):
                                    for nn2 in range(2):
                                        bank = tt * 2 + nn2
                                        v_mm = ins("t", lambda e, sl=slot, k=kcc, t_=th * 2 + tt,
                                                   n_=nh * 2 + nn2, b=bank:
                                                   e.matmul(PSF[:, b * 512:(b + 1) * 512],
                                                            att_grp[:, k * 512 + t_ * 128: k * 512 + (t_ + 1) * 128],
                                                            ring_a[:, sl * 2048 + n_ * 512: sl * 2048 + (n_ + 1) * 512],
                                                            start=(k == 0), stop=(k == 15)),
                                                   tm if (tt == 1 and nn2 == 1) else None, 1)
                                ring_free(tm, v_mm)
                            wait("v", tm, v_mm)
                            v_xu = 0
                            for tt in range(2):
                                ti = grp * 4 + th * 2 + tt
                                for nn2 in range(2):
                                    b = tt * 2 + nn2
                                    n_ = nh * 2 + nn2
                                    v_xu = ins("v", lambda e, b_=b, n_=n_, ti_=ti:
                                               e.scalar_tensor_tensor(
                                        out=X[:, ti_ * 2048 + n_ * 512: ti_ * 2048 + (n_ + 1) * 512],
                                        in0=PSF[:, b_ * 512:(b_ + 1) * 512], scalar=1.0,
                                        in1=X[:, ti_ * 2048 + n_ * 512: ti_ * 2048 + (n_ + 1) * 512],
                                        op0=ALU.mult, op1=ALU.add), vm)
                            wait("t", vm, v_xu)
                    barrier()
        barrier()



    def sub_moe(app):
        tag = f"m{app}"
        dbgtgt = [0]
        with (
            nc.sbuf_tensor(f"hn{tag}", [128, DC * 1024], BF16) as hn_fm,
            nc.sbuf_tensor(f"ring{tag}", [128, 14 * 2048], BF16) as ring_m,
            nc.sbuf_tensor(f"an{tag}", [128, 11 * 1024], BF16) as a_n,
            nc.sbuf_tensor(f"wb{tag}", [128, 1024], F32) as wb,
            nc.sbuf_tensor(f"wT{tag}", [8, 1024], F32) as w_T,
            nc.sbuf_tensor(f"rt{tag}", [128, 8 * 40], F32) as RTS,
            nc.sbuf_tensor(f"gwrf{tag}", [128, 128], F32) as GWRF,
            nc.sbuf_tensor(f"sel{tag}", [8, 1024], F32) as SEL8,
            nc.sbuf_tensor(f"lnst{tag}", [128, 16], F32) as LNST,
            nc.sbuf_tensor(f"xcol{tag}", [128, 256], F32) as XCOL,
        ):
            ring_begin(ring_m, 14)
            ins("y", lambda e: e.dma_start(out=GWRF[:], in_=DT["gwr"][:, :]), sq, 16)
            v_gw = ins("y", lambda e: e.dma_start(out=SEL8[0:8, :], in_=DT["sel8"][:, :]), sq, 16)
            wait("t", sq, v_gw)
            wait("v", sq, v_gw)
            with nc.sbuf_tensor(f"scb2{tag}", [128, 2048], BF16) as scb2m:
                psb = PSB
                for i in range(0, NT, 2):
                    emit_ln_pair((X[:, i * 2048:(i + 1) * 2048],
                                  X[:, (i + 1) * 2048:(i + 2) * 2048]),
                                 hn_fm, 1024, (i * 128, (i + 1) * 128), psb, scb2m,
                                 stats=((LNST[:, 2 * i:2 * i + 1],
                                         LNST[:, 2 * i + 1:2 * i + 2]),
                                        (LNST[:, 2 * i + 2:2 * i + 3],
                                         LNST[:, 2 * i + 3:2 * i + 4])))
                barrier()
            if True:
                psR = PSF[:, 5 * 512: 6 * 512]
                psT = PSF[:, 0:1024]
                v_lg = None
                for t in range(8):
                    v_xc = ins("v", lambda e, t=t: e.tensor_scalar(
                        out=SCR[:, 0:2048], in0=X[:, t * 2048:(t + 1) * 2048],
                        scalar1=LNST[:, 2 * t:2 * t + 1], scalar2=None,
                        op0=ALU.subtract), vm)
                    wait("t", vm, v_xc)
                    evv = [0, 0]
                    for kc in range(16):
                        sl = kc % 2
                        if kc >= 2:
                            wait("t", vm, evv[sl])
                        v_tp = ins("t", lambda e, k=kc, sl=sl: e.transpose(
                            psT[:, sl * 512: sl * 512 + 128],
                            SCR[:, k * 128:(k + 1) * 128], IDF[:]), tm)
                        wait("v", tm, v_tp)
                        evv[sl] = ins("v", lambda e, sl=sl: e.tensor_copy(
                            out=XCOL[:, sl * 128:(sl + 1) * 128],
                            in_=psT[:, sl * 512: sl * 512 + 128]), vm)
                        wait("t", vm, evv[sl])
                        v_lg = ins("t", lambda e, t=t, k=kc, sl=sl: e.matmul(
                            psR[:, t * 8:(t + 1) * 8],
                            XCOL[:, sl * 128:(sl + 1) * 128],
                            GWRF[:, k * 8:(k + 1) * 8],
                            start=(k == 0), stop=(k == 15)),
                            tm if kc == 15 else None, 1)
                wait("v", tm, v_lg)
                wait("s", tm, v_lg)

                def rc(t, q):
                    return RTS[:, t * 40 + q: t * 40 + q + 1]

                def rr(t, a, bcount):
                    return RTS[:, t * 40 + a: t * 40 + a + bcount]

                for t in range(8):
                    em("v", lambda e, t=t: e.reduce_max(
                        out=rc(t, 0), in_=psR[:, t * 8:(t + 1) * 8], axis=AX.X,
                        negate=True))
                v_ng = None
                for t in range(8):
                    v_ng = ins("v", lambda e, t=t: e.tensor_scalar(
                        out=rc(t, 0), in0=rc(t, 0),
                        scalar1=LNST[:, 2 * t + 1:2 * t + 2], scalar2=None,
                        op0=ALU.mult), vm if t == 7 else None)
                wait("s", vm, v_ng)
                v_sc = None
                for t in range(8):
                    v_sc = ins("s", lambda e, t=t: e.activation(
                        out=rr(t, 4, 8), in_=psR[:, t * 8:(t + 1) * 8], func=AF.Exp,
                        scale=LNST[:, 2 * t + 1:2 * t + 2],
                        bias=rc(t, 0), accum_out=rc(t, 1)), am)
                wait("v", am, v_sc)
                for t in range(8):
                    em("v", lambda e, t=t: e.reciprocal(out=rc(t, 2), in_=rc(t, 1)))
                for t in range(8):
                    em("v", lambda e, t=t: e.tensor_scalar(
                        out=rr(t, 4, 8), in0=rr(t, 4, 8), scalar1=rc(t, 2),
                        scalar2=None, op0=ALU.mult))
                for t in range(8):
                    em("v", lambda e, t=t: e.tensor_tensor(
                        out=rr(t, 12, 4),
                        in0=RTS.ap().rearrange("p (b two) -> p b two", two=2)[
                            :, t * 20 + 2: t * 20 + 6, 0],
                        in1=RTS.ap().rearrange("p (b two) -> p b two", two=2)[
                            :, t * 20 + 2: t * 20 + 6, 1],
                        op=ALU.max))
                for t in range(8):
                    em("v", lambda e, t=t: e.tensor_copy(out=rr(t, 16, 4),
                                                         in_=RNK0[:, 4:8]))
                for o in range(1, 4):
                    for t in range(8):
                        em("v", lambda e, t=t, o=o: e.tensor_tensor(
                            out=rr(t, 36, 4 - o), in0=rr(t, 12, 4 - o),
                            in1=rr(t, 12 + o, 4 - o), op=ALU.is_ge))
                    for t in range(8):
                        em("v", lambda e, t=t, o=o: e.tensor_tensor(
                            out=rr(t, 16 + o, 4 - o), in0=rr(t, 16 + o, 4 - o),
                            in1=rr(t, 36, 4 - o), op=ALU.add))
                    for t in range(8):
                        em("v", lambda e, t=t, o=o: e.tensor_tensor(
                            out=rr(t, 16, 4 - o), in0=rr(t, 16, 4 - o),
                            in1=rr(t, 36, 4 - o), op=ALU.subtract))
                for t in range(8):
                    em("v", lambda e, t=t: e.tensor_scalar(
                        out=rr(t, 36, 4), in0=rr(t, 16, 4), scalar1=1.5,
                        scalar2=None, op0=ALU.is_le))
                for par in range(2):
                    for t in range(8):
                        em("v", lambda e, t=t, par=par: e.tensor_tensor(
                            out=RTS.ap().rearrange("p (b two) -> p b two", two=2)[
                                :, t * 20 + 10: t * 20 + 14, par],
                            in0=RTS.ap().rearrange("p (b two) -> p b two", two=2)[
                                :, t * 20 + 2: t * 20 + 6, par],
                            in1=rr(t, 36, 4), op=ALU.mult))
                for t in range(8):
                    em("v", lambda e, t=t: e.tensor_copy(out=rr(t, 28, 8),
                                                         in_=RNK0[:, 0:8]))
                for o in range(1, 8):
                    for t in range(8):
                        em("v", lambda e, t=t, o=o: e.tensor_tensor(
                            out=rr(t, 12, 8 - o), in0=rr(t, 20, 8 - o),
                            in1=rr(t, 20 + o, 8 - o), op=ALU.is_ge))
                    for t in range(8):
                        em("v", lambda e, t=t, o=o: e.tensor_tensor(
                            out=rr(t, 28 + o, 8 - o), in0=rr(t, 28 + o, 8 - o),
                            in1=rr(t, 12, 8 - o), op=ALU.add))
                    for t in range(8):
                        em("v", lambda e, t=t, o=o: e.tensor_tensor(
                            out=rr(t, 28, 8 - o), in0=rr(t, 28, 8 - o),
                            in1=rr(t, 12, 8 - o), op=ALU.subtract))
                for t in range(8):
                    em("v", lambda e, t=t: e.tensor_scalar(
                        out=rr(t, 12, 8), in0=rr(t, 28, 8), scalar1=1.5,
                        scalar2=None, op0=ALU.is_le))
                v_w = None
                for t in range(8):
                    v_w = ins("v", lambda e, t=t: e.tensor_tensor(
                        out=rr(t, 4, 8), in0=rr(t, 20, 8), in1=rr(t, 12, 8),
                        op=ALU.mult), vm)
                wait("t", vm, v_w)
                v_wt = [0, 0]
                for t in range(8):
                    if v_wt[t % 2]:
                        wait("t", vm, v_wt[t % 2])
                    v_tp = ins("t", lambda e, t=t: e.transpose(
                        psT[0:8, (t % 2) * 512: (t % 2) * 512 + 128],
                        rr(t, 4, 8), IDF[:]), tm)
                    wait("v", tm, v_tp)
                    v_wt[t % 2] = ins("v", lambda e, t=t: e.tensor_copy(
                        out=w_T[0:8, t * 128:(t + 1) * 128],
                        in_=psT[0:8, (t % 2) * 512: (t % 2) * 512 + 128]), vm)
                barrier()
                if STAGE == 3 and app == 1:
                    dump(hn_fm[:, 0:16384], 16384)
                    dump(RTS[:, 0:320], 320)
            if True:
                for ex in range(10):
                    routed = ex < 8
                    if routed:
                        gsrc, usrc, dsrc = DT["gws"], DT["uws"], DT["dws"]
                        fb0 = ex * 11
                    else:
                        gsrc, usrc, dsrc = DT["sgs"], DT["sus"], DT["sds"]
                        fb0 = (ex - 8) * 11
                    if routed:
                        v_bc = None
                        for nn2 in range(2):
                            v_bc = ins("t", lambda e, n=nn2, ex=ex: e.matmul(
                                PSF[:, (4 + n) * 512: (5 + n) * 512],
                                SEL8[0:8, ex * 128:(ex + 1) * 128],
                                w_T[0:8, n * 512:(n + 1) * 512],
                                start=True, stop=True), tm)
                        wait("v", tm, v_bc)
                        v_wb = None
                        for nn2 in range(2):
                            v_wb = ins("v", lambda e, n=nn2: e.tensor_copy(
                                out=wb[:, n * 512:(n + 1) * 512],
                                in_=PSF[:, (4 + n) * 512: (5 + n) * 512]),
                                vm)
                        wait("t", vm, v_wb)
                    aev = [0, 0]
                    for nh in range(2):
                        for fb in range(11):
                            bs = (fb % 2) * 2
                            if aev[fb % 2]:
                                wait("t", vm, aev[fb % 2])
                            v_gu = None
                            for gu, wsrc in ((0, gsrc), (1, usrc)):
                                slot, tgt = ring_load(wsrc[(fb0 + fb) * 128:(fb0 + fb + 1) * 128, :])
                                wait("t", gq, tgt)
                                for kc in range(16):
                                    v_gu = ins("t", lambda e, sl=slot, k=kc, n=nh, b=bs + gu:
                                               e.matmul(PSF[:, b * 512:(b + 1) * 512],
                                                        ring_m[:, sl * 2048 + k * 128: sl * 2048 + (k + 1) * 128],
                                                        hn_fm[:, k * 1024 + n * 512: k * 1024 + n * 512 + 512],
                                                        start=(k == 0), stop=(k == 15)),
                                               tm if kc == 15 else None, 1)
                                ring_free(tm, v_gu)
                            wait("s", tm, v_gu)
                            v_si = ins("s", lambda e, b=bs: e.activation(
                                out=SCR[:, 0:512],
                                in_=PSF[:, b * 512:(b + 1) * 512],
                                func=AF.Silu), am)
                            wait("v", am, v_si)
                            em("v", lambda e, b=bs: e.tensor_tensor(
                                out=SCR[:, 1024:1536],
                                in0=SCR[:, 0:512],
                                in1=PSF[:, (b + 1) * 512:(b + 2) * 512],
                                op=ALU.mult))
                            if routed:
                                aev[fb % 2] = ins("v", lambda e, n=nh, fb=fb: e.tensor_tensor(
                                    out=a_n[:, fb * 1024 + n * 512: fb * 1024 + (n + 1) * 512],
                                    in0=SCR[:, 1024:1536],
                                    in1=wb[:, n * 512:(n + 1) * 512], op=ALU.mult), vm)
                            else:
                                aev[fb % 2] = ins("v", lambda e, n=nh, fb=fb: e.tensor_copy(
                                    out=a_n[:, fb * 1024 + n * 512: fb * 1024 + (n + 1) * 512],
                                    in_=SCR[:, 1024:1536]), vm)
                    wait("t", vm, max(aev))
                    dslots = []
                    for k in range(11):
                        slot, tgt = ring_load(dsrc[(fb0 + k) * 128:(fb0 + k + 1) * 128, :])
                        dslots.append((slot, tgt))
                    xev = [0, 0]
                    dn_last = {}
                    for nn2 in range(4):
                        for t in range(8):
                            ci = nn2 * 8 + t
                            bank = ci % 2
                            if ci >= 2:
                                wait("t", vm, xev[bank])
                            v_dn = None
                            for k in range(11):
                                slot, tgt = dslots[k]
                                if ci == 0:
                                    wait("t", gq, tgt)
                                v_dn = ins("t", lambda e, sl=slot, k=k, n=nn2, t=t, b=bank:
                                           e.matmul(PSF[:, (4 + b) * 512:(5 + b) * 512],
                                                    a_n[:, k * 1024 + t * 128: k * 1024 + (t + 1) * 128],
                                                    ring_m[:, sl * 2048 + n * 512: sl * 2048 + (n + 1) * 512],
                                                    start=(k == 0), stop=(k == 10)),
                                           tm if k == 10 else None, 1)
                                if ci == 31:
                                    dn_last[k] = v_dn
                            wait("v", tm, v_dn)
                            xev[bank] = ins("v", lambda e, n=nn2, t=t, b=bank:
                                            e.scalar_tensor_tensor(
                                out=X[:, t * 2048 + n * 512: t * 2048 + (n + 1) * 512],
                                in0=PSF[:, (4 + b) * 512:(5 + b) * 512], scalar=1.0,
                                in1=X[:, t * 2048 + n * 512: t * 2048 + (n + 1) * 512],
                                op0=ALU.mult, op1=ALU.add), vm)
                    for k in range(11):
                        ring_free(tm, dn_last[10])
                    wait("t", vm, max(xev))
                    if STAGE == 3 and app == 1 and ex == 0:
                        wait("g", vm, max(aev))
                        dump(a_n[:, 0:11264], 11264)
                        dump(wb[:, 0:1024], 1024)
                        o = dbg_off[0]
                        em("g", lambda e, oo=o: e.dma_start(
                            out=dbg_d[0:8, oo: oo + 1024], in_=w_T[0:8, :]).then_inc(gq.h, 16))
                        gq.inc(16)
                        dbg_off[0] += 1024
                        dbgtgt[0] = gq.v
                    if STAGE == 3 and app == 1 and ex == 1 and dbgtgt[0]:
                        wait("v", gq, dbgtgt[0])
                barrier()



    LN16 = float(np.log(16.0))

    def sub_gla():
        import contextlib as _ctl
        gst = _ctl.ExitStack()
        xn3 = gst.enter_context(nc.sbuf_tensor("g_xn", [128, DC * 1024], BF16))
        ring_g = gst.enter_context(nc.sbuf_tensor("g_ring", [128, 3 * 2048], BF16))
        S = gst.enter_context(nc.sbuf_tensor("g_S", [128, 8 * 256], F32))
        SB16 = gst.enter_context(nc.sbuf_tensor("g_S16", [128, 8 * 256], BF16))
        DL = gst.enter_context(nc.sbuf_tensor("g_DL", [128, 64], F32))
        SUML = gst.enter_context(nc.sbuf_tensor("g_SUML", [128, 8], F32))
        z1b = gst.enter_context(nc.sbuf_tensor("g_z1b", [16, 1024], BF16))
        BKG1 = gst.enter_context(nc.sbuf_tensor("g_kg1", [128, 256], BF16))
        BKG2 = gst.enter_context(nc.sbuf_tensor("g_kg2", [16, 1024], BF16))
        KG2B = gst.enter_context(nc.sbuf_tensor("g_kg2b", [128, 8], F32))
        MV = gst.enter_context(nc.sbuf_tensor("g_mv", [128, 8], F32))
        MVI = gst.enter_context(nc.sbuf_tensor("g_mvi", [128, 8], F32))
        BGBB = gst.enter_context(nc.sbuf_tensor("g_bgb", [128, 2048], BF16))
        GZ = gst.enter_context(nc.sbuf_tensor("g_gz", [128, 128], F32))
        KDT = gst.enter_context(nc.sbuf_tensor("g_kdt", [128, 8 * 4 * 128], BF16))
        VT = gst.enter_context(nc.sbuf_tensor("g_vt", [128, 4 * 2048], BF16))
        if True:
            ring_begin(ring_g, 3)
            ins("g", lambda e: e.dma_start(out=BKG1[:], in_=DT["bkg1"][:, :]), gq, 16)
            ins("g", lambda e: e.dma_start(out=BKG2[0:16, :], in_=DT["bkg2"][:, :]), gq, 16)
            v_gg = ins("g", lambda e: e.dma_start(out=BGBB[:], in_=DT["bgb"][:, :]), gq, 16)
            ins("y", lambda e: e.dma_start(out=KG2B[:], in_=DT["bkg2b"][:, :]), sq, 16)
            ins("y", lambda e: e.dma_start(out=MV[:], in_=DT["mvec"][:, :]), sq, 16)
            v_mv = ins("y", lambda e: e.dma_start(out=MVI[:], in_=DT["mvi"][:, :]), sq, 16)
            ins("v", lambda e: e.memset(S[:], 0.0))
            ins("v", lambda e: e.memset(GZ[:], 0.0))
            v_ini = ins("v", lambda e: e.memset(SUML[:], 0.0), vm)
            with nc.sbuf_tensor("g_scb2", [128, 2048], BF16) as scb2g:
                psb = PSB
                for i in range(0, NT, 2):
                    emit_ln_pair((X[:, i * 2048:(i + 1) * 2048],
                                  X[:, (i + 1) * 2048:(i + 2) * 2048]),
                                 xn3, 1024, (i * 128, (i + 1) * 128), psb, scb2g)
                barrier()
            if True:
                ps0 = PSF
                v_z1 = None
                for n2 in range(2):
                    for kc in range(16):
                        v_z1 = ins("t", lambda e, k=kc, n=n2: e.matmul(
                            ps0[0:16, n * 512:(n + 1) * 512],
                            BKG1[:, k * 16:(k + 1) * 16],
                            xn3[:, k * 1024 + n * 512: k * 1024 + n * 512 + 512],
                            start=(k == 0), stop=(k == 15)),
                            tm if kc == 15 else None, 1)
                wait("v", tm, v_z1)
                v_z1b = ins("v", lambda e: e.tensor_copy(
                    out=z1b[0:16, :], in_=ps0[0:16, 0:1024]), vm)
                wait("t", vm, v_z1b)
            barrier()

            def emit_gk(h, half, psZ2, slot, guard_am, GKC, SCRB):
                v_z2 = ins("t", lambda e, h=h, half=half, sl=slot: e.matmul(
                    psZ2[:, sl * 512: sl * 512 + 512],
                    BKG2[0:16, h * 128:(h + 1) * 128],
                    z1b[0:16, half * 512: half * 512 + 512],
                    start=True, stop=True), tm)
                wait("s", tm, v_z2)
                ins("s", lambda e, h=h, sl=slot: e.activation(
                    out=SCR[:, 0:512], in_=psZ2[:, sl * 512: sl * 512 + 512],
                    func=AF.Sigmoid, bias=KG2B[:, h:h + 1]), am)
                v_ln = ins("s", lambda e: e.activation(
                    out=SCR[:, 512:1024], in_=SCR[:, 0:512], func=AF.Ln), am)
                wait("t", am, v_ln)
                wait("v", am, v_ln)
                if guard_am[slot]:
                    wait("v", am, guard_am[slot])
                v_sc = None
                for c in range(4):
                    v_sc = ins("v", lambda e, c=c, sl=slot: e.tensor_tensor_scan(
                        out=GKC[:, sl * 512 + c * 128: sl * 512 + (c + 1) * 128],
                        data0=SCR[:, 512 + c * 128: 512 + (c + 1) * 128],
                        data1=GZ[:, 0:128], initial=0.0,
                        op0=ALU.add, op1=ALU.add), vm)
                return v_sc

            def gkc_last(GKC, slot, c):
                return GKC[:, slot * 512 + c * 128 + 127: slot * 512 + c * 128 + 128]

            def emit_vproj(half, tag2):
                if True:
                    combos = [(s, n) for s in range(4) for n in range(4)]
                    for p0 in range(0, 16, 6):
                        sub = combos[p0:p0 + 6]
                        v_vm = None
                        for kc in range(16):
                            slot, tgt = ring_load(DT["bv"][kc * 128:(kc + 1) * 128, :])
                            wait("t", gq, tgt)
                            for bi, (s4, n4) in enumerate(sub):
                                v_vm = ins("t", lambda e, sl=slot, k=kc, s=s4, n=n4, b=bi, hf=half:
                                           e.matmul(PSF[:, b * 512:(b + 1) * 512],
                                                    xn3[:, k * 1024 + hf * 512 + s * 128: k * 1024 + hf * 512 + (s + 1) * 128],
                                                    ring_g[:, sl * 2048 + n * 512: sl * 2048 + (n + 1) * 512],
                                                    start=(k == 0), stop=(k == 15)),
                                           tm if bi == len(sub) - 1 else None, 1)
                            ring_free(tm, v_vm)
                        wait("v", tm, v_vm)
                        v_ve = None
                        for bi, (s4, n4) in enumerate(sub):
                            v_ve = ins("v", lambda e, s=s4, n=n4, b=bi: e.tensor_copy(
                                out=VT[:, s * 2048 + n * 512: s * 2048 + (n + 1) * 512],
                                in_=PSF[:, b * 512:(b + 1) * 512]), vm)
                        wait("t", vm, v_ve)

            wait("v", sq, v_mv)
            for half in range(2):
                with (
                    nc.sbuf_tensor(f"g_gkc1{half}", [128, 2 * 512], F32) as GKC,
                    nc.sbuf_tensor(f"g_scb1{half}", [128, 512], BF16) as SCRB,
                ):
                    psZ2 = PSF[:, 0:1024]
                    psK = PSF[:, 1024:2048]
                    psbT = PSB
                    guard = [0, 0]
                    kguard = [0, 0]
                    kdev = [0, 0]
                    for h in range(8):
                        sl = h % 2
                        v_sc = emit_gk(h, half, psZ2, sl, guard, GKC, SCRB)
                        slot, tgt = ring_load(DT["bk"][h * 128:(h + 1) * 128, :])
                        wait("t", gq, tgt)
                        if kguard[sl]:
                            wait("t", vm, kguard[sl])
                        v_k = None
                        for kc in range(16):
                            v_k = ins("t", lambda e, sll=slot, k=kc, hf=half, sl=sl: e.matmul(
                                psK[:, sl * 512: sl * 512 + 512],
                                ring_g[:, sll * 2048 + k * 128: sll * 2048 + (k + 1) * 128],
                                xn3[:, k * 1024 + hf * 512: k * 1024 + hf * 512 + 512],
                                start=(k == 0), stop=(k == 15)),
                                tm if kc == 15 else None, 1)
                        ring_free(tm, v_k)
                        wait("v", tm, v_k)
                        wait("v", vm, v_sc)
                        v_kd = None
                        v_ed = None
                        for c in range(4):
                            em("v", lambda e, lc=gkc_last(GKC, sl, c): e.tensor_scalar(
                                out=tc(20), in0=lc, scalar1=1.0 / 16, scalar2=-LN16,
                                op0=ALU.mult, op1=ALU.add))
                            em("v", lambda e, h=h, lc=gkc_last(GKC, sl, c): e.tensor_tensor(
                                out=SUML[:, h:h + 1], in0=SUML[:, h:h + 1], in1=lc,
                                op=ALU.add))
                            v_t21 = ins("v", lambda e, lc=gkc_last(GKC, sl, c): e.tensor_scalar(
                                out=tc(21), in0=lc, scalar1=1.0 / 16, scalar2=None,
                                op0=ALU.mult), vm)
                            wait("s", vm, v_t21)
                            v_ed = ins("s", lambda e, c=c, sl=sl, SCRB=SCRB, GKC=GKC: e.activation(
                                out=SCRB[:, c * 128:(c + 1) * 128],
                                in_=GKC[:, sl * 512 + c * 128: sl * 512 + (c + 1) * 128],
                                func=AF.Exp, scale=-1.0 / 16, bias=tc(20)), am)
                            ins("s", lambda e, h=h, hf=half, c=c: e.activation(
                                out=DL[:, h * 8 + hf * 4 + c: h * 8 + hf * 4 + c + 1],
                                in_=tc(21), func=AF.Exp), am)
                            wait("v", am, v_ed)
                            v_kd = ins("v", lambda e, c=c, sl=sl, psK=psK, SCRB=SCRB: e.tensor_tensor(
                                out=SCB[:, c * 128:(c + 1) * 128],
                                in0=psK[:, sl * 512 + c * 128: sl * 512 + (c + 1) * 128],
                                in1=SCRB[:, c * 128:(c + 1) * 128], op=ALU.mult), vm)
                        guard[sl] = am.v
                        kguard[sl] = v_kd
                        wait("t", vm, v_kd)
                        for c in range(4):
                            hfp = (c % 2) * 1024
                            if kdev[c % 2]:
                                wait("t", vm, kdev[c % 2])
                            v_tp = ins("t", lambda e, c=c, hfp=hfp: e.transpose(
                                psbT[:, hfp: hfp + 128], SCB[:, c * 128:(c + 1) * 128],
                                IDB[:]), tm)
                            wait("v", tm, v_tp)
                            kdev[c % 2] = ins("v", lambda e, h=h, c=c, hfp=hfp: e.tensor_copy(
                                out=KDT[:, (h * 4 + c) * 128:(h * 4 + c + 1) * 128],
                                in_=psbT[:, hfp: hfp + 128]), vm)
                        wait("t", vm, max(kdev))
                    barrier()
                emit_vproj(half, f"1{half}")
                barrier()
                if True:
                    psKV = PSF[:, 0:1024]
                    sev = [0, 0]
                    for c in range(4):
                        for h in range(8):
                            sl2 = h % 2
                            if sev[sl2]:
                                wait("t", vm, sev[sl2])
                            v_kv = ins("t", lambda e, h=h, c=c, sl2=sl2: e.matmul(
                                psKV[:, sl2 * 512: sl2 * 512 + 256],
                                KDT[:, (h * 4 + c) * 128:(h * 4 + c + 1) * 128],
                                VT[:, c * 2048 + h * 256: c * 2048 + (h + 1) * 256],
                                start=True, stop=True), tm)
                            wait("v", tm, v_kv)
                            sev[sl2] = ins("v", lambda e, h=h, hf=half, c=c, sl2=sl2:
                                           e.scalar_tensor_tensor(
                                out=S[:, h * 256:(h + 1) * 256],
                                in0=S[:, h * 256:(h + 1) * 256],
                                scalar=DL[:, h * 8 + hf * 4 + c: h * 8 + hf * 4 + c + 1],
                                in1=psKV[:, sl2 * 512: sl2 * 512 + 256],
                                op0=ALU.mult, op1=ALU.add), vm)
                    wait("t", vm, max(sev))
                barrier()

            if GLA_CUT >= 2:
                with nc.sbuf_tensor("g_exb", [128, 8 * 257], F32) as EXB:
                    for h in range(8):
                        em("v", lambda e, h=h: e.tensor_copy(
                            out=EXB[:, h * 257: h * 257 + 256],
                            in_=S[:, h * 256:(h + 1) * 256]))
                    v_ex = ins("v", lambda e: e.tensor_scalar(
                        out=TIN[:, 24:32], in0=SUML[:, 0:8], scalar1=1.0 / 16,
                        scalar2=None, op0=ALU.mult), vm)
                    wait("s", vm, v_ex)
                    v_ex2 = ins("s", lambda e: e.activation(
                        out=EXB.ap().rearrange("p (h q) -> p h q", q=257)[:, :, 256:257],
                        in_=TIN[:, 24:32], func=AF.Exp), am)
                    wait("y", vm, v_ex)
                    wait("y", am, v_ex2)
                    v_out = ins("y", lambda e: e.dma_start(out=cci_d[:, :], in_=EXB[:]), sq, 16)
                    wait("g", sq, v_out)
                    v_cc = ins("g", lambda e: e.collective_compute(
                        "AllGather", ALU.bypass,
                        replica_groups=[list(range(NCORE))],
                        ins=[cci_d.tensor.ap()], outs=[cco_d.tensor.ap()]), cc)
                    wait("v", cc, v_cc)
                    v_s0 = ins("v", lambda e: e.memset(S[:], 0.0), vm)

                with nc.sbuf_tensor("g_gin", [128, 2 * 2056], F32) as GIN:
                    cmb = [0, 0]
                    wait("y", cc, cc.v)
                    for j in range(NCORE):
                        gsl = j % 2
                        if j >= 2:
                            wait("y", vm, cmb[gsl])
                        v_gl = ins("y", lambda e, j=j, gsl=gsl: e.dma_start(
                            out=GIN[:, gsl * 2056:(gsl + 1) * 2056],
                            in_=cco_d[j * 128:(j + 1) * 128, :]), sq, 16)
                        wait("v", sq, v_gl)
                        if j == 0:
                            wait("v", vm, v_s0)
                        for h in range(8):
                            em("v", lambda e, j=j, h=h, gsl=gsl: e.tensor_scalar(
                                out=tc(24 + h), in0=GIN[:, gsl * 2056 + h * 257 + 256: gsl * 2056 + h * 257 + 257],
                                scalar1=MV[:, j:j + 1], scalar2=MVI[:, j:j + 1],
                                op0=ALU.mult, op1=ALU.add))
                        for h in range(8):
                            em("v", lambda e, h=h: e.tensor_scalar(
                                out=S[:, h * 256:(h + 1) * 256],
                                in0=S[:, h * 256:(h + 1) * 256],
                                scalar1=tc(24 + h), scalar2=None, op0=ALU.mult))
                            cmb[gsl] = ins("v", lambda e, j=j, h=h, gsl=gsl: e.scalar_tensor_tensor(
                                out=S[:, h * 256:(h + 1) * 256],
                                in0=GIN[:, gsl * 2056 + h * 257: gsl * 2056 + h * 257 + 256],
                                scalar=MV[:, j:j + 1],
                                in1=S[:, h * 256:(h + 1) * 256],
                                op0=ALU.mult, op1=ALU.add), vm)
                    v_sb = None
                    for h in range(8):
                        v_sb = ins("v", lambda e, h=h: e.tensor_copy(
                            out=SB16[:, h * 256:(h + 1) * 256],
                            in_=S[:, h * 256:(h + 1) * 256]), vm)
                    wait("t", vm, v_sb)
                barrier()
                if STAGE == 4:
                    dump(S[:, 0:2048], 2048)
                    dump(DL[:, 0:64], 64)
                    barrier()


            if GLA_CUT >= 3:
                for half in range(2):
                    with (
                        nc.sbuf_tensor(f"g_qd{half}", [128, 8 * 512], BF16) as QD,
                        nc.sbuf_tensor(f"g_kn{half}", [128, 8 * 512], BF16) as KN,
                        nc.sbuf_tensor(f"g_gh{half}", [128, 4 * 2048], BF16) as GH,
                    ):
                        with (
                            nc.sbuf_tensor(f"g_gkc2{half}", [128, 2 * 512], F32) as GKC,
                            nc.sbuf_tensor(f"g_scb2h{half}", [128, 512], BF16) as SCRB,
                        ):
                            psZ2 = PSF[:, 0:1024]
                            psK = PSF[:, 1024:2048]
                            psQ = PSF[:, 2048:3072]
                            psbT = PSB
                            guard = [0, 0]
                            kguard = [0, 0]
                            qguard = [0, 0]
                            kdev = [0, 0]
                            for h in range(8):
                                sl = h % 2
                                v_sc = emit_gk(h, half, psZ2, sl, guard, GKC, SCRB)
                                wait("s", vm, v_sc)
                                v_en = ins("s", lambda e, sl=sl, GKC=GKC: e.activation(
                                    out=SCR[:, 1024:1536], in_=GKC[:, sl * 512: sl * 512 + 512],
                                    func=AF.Exp, scale=-1.0 / 16, bias=tc(14)), am)
                                v_ep = ins("s", lambda e, sl=sl, GKC=GKC: e.activation(
                                    out=SCR[:, 1536:2048], in_=GKC[:, sl * 512: sl * 512 + 512],
                                    func=AF.Exp, scale=1.0 / 16), am)
                                slot, tgt = ring_load(DT["bk"][h * 128:(h + 1) * 128, :])
                                wait("t", gq, tgt)
                                if kguard[sl]:
                                    wait("t", vm, kguard[sl])
                                v_k = None
                                for kc in range(16):
                                    v_k = ins("t", lambda e, sll=slot, k=kc, hf=half, sl=sl: e.matmul(
                                        psK[:, sl * 512: sl * 512 + 512],
                                        ring_g[:, sll * 2048 + k * 128: sll * 2048 + (k + 1) * 128],
                                        xn3[:, k * 1024 + hf * 512: k * 1024 + hf * 512 + 512],
                                        start=(k == 0), stop=(k == 15)),
                                        tm if kc == 15 else None, 1)
                                ring_free(tm, v_k)
                                wait("v", tm, v_k)
                                wait("v", am, v_en)
                                v_kn = ins("v", lambda e, h=h, sl=sl, KN=KN, psK=psK: e.tensor_tensor(
                                    out=KN[:, h * 512:(h + 1) * 512],
                                    in0=psK[:, sl * 512: sl * 512 + 512],
                                    in1=SCR[:, 1024:1536], op=ALU.mult), vm)
                                v_kd = None
                                for c in range(4):
                                    v_b = ins("v", lambda e, lc=gkc_last(GKC, sl, c): e.tensor_scalar(
                                        out=tc(20), in0=lc, scalar1=1.0 / 16, scalar2=-LN16,
                                        op0=ALU.mult, op1=ALU.add), vm)
                                    wait("s", vm, v_b)
                                    v_ed = ins("s", lambda e, c=c, sl=sl, SCRB=SCRB, GKC=GKC: e.activation(
                                        out=SCRB[:, c * 128:(c + 1) * 128],
                                        in_=GKC[:, sl * 512 + c * 128: sl * 512 + (c + 1) * 128],
                                        func=AF.Exp, scale=-1.0 / 16, bias=tc(20)), am)
                                    wait("v", am, v_ed)
                                    v_kd = ins("v", lambda e, c=c, sl=sl, psK=psK, SCRB=SCRB: e.tensor_tensor(
                                        out=SCB[:, c * 128:(c + 1) * 128],
                                        in0=psK[:, sl * 512 + c * 128: sl * 512 + (c + 1) * 128],
                                        in1=SCRB[:, c * 128:(c + 1) * 128], op=ALU.mult), vm)
                                kguard[sl] = v_kd
                                wait("t", vm, v_kd)
                                for c in range(4):
                                    hfp = (c % 2) * 1024
                                    if kdev[c % 2]:
                                        wait("t", vm, kdev[c % 2])
                                    v_tp = ins("t", lambda e, c=c, hfp=hfp: e.transpose(
                                        psbT[:, hfp: hfp + 128], SCB[:, c * 128:(c + 1) * 128],
                                        IDB[:]), tm)
                                    wait("v", tm, v_tp)
                                    kdev[c % 2] = ins("v", lambda e, h=h, c=c, hfp=hfp: e.tensor_copy(
                                        out=KDT[:, (h * 4 + c) * 128:(h * 4 + c + 1) * 128],
                                        in_=psbT[:, hfp: hfp + 128]), vm)
                                wait("t", vm, max(kdev))
                                slot, tgt = ring_load(DT["bq"][h * 128:(h + 1) * 128, :])
                                wait("t", gq, tgt)
                                if qguard[sl]:
                                    wait("t", vm, qguard[sl])
                                v_q = None
                                for kc in range(16):
                                    v_q = ins("t", lambda e, sll=slot, k=kc, hf=half, sl=sl: e.matmul(
                                        psQ[:, sl * 512: sl * 512 + 512],
                                        ring_g[:, sll * 2048 + k * 128: sll * 2048 + (k + 1) * 128],
                                        xn3[:, k * 1024 + hf * 512: k * 1024 + hf * 512 + 512],
                                        start=(k == 0), stop=(k == 15)),
                                        tm if kc == 15 else None, 1)
                                ring_free(tm, v_q)
                                wait("v", tm, v_q)
                                wait("v", am, v_ep)
                                qguard[sl] = ins("v", lambda e, h=h, sl=sl, QD=QD, psQ=psQ: e.tensor_tensor(
                                    out=QD[:, h * 512:(h + 1) * 512],
                                    in0=psQ[:, sl * 512: sl * 512 + 512],
                                    in1=SCR[:, 1536:2048], op=ALU.mult), vm)
                                guard[sl] = am.v
                            wait("t", vm, max(qguard))
                            barrier()
                            if STAGE == 4 and half == 0:
                                dump(QD[:, 0:4096], 4096)
                                dump(KN[:, 0:4096], 4096)
                                dump(KDT[:, 0:4096], 4096)
                                barrier()
                        emit_vproj(half, f"2{half}")
                        barrier()
                        if STAGE == 4 and half == 0:
                            dump(VT[:, 0:8192], 8192)
                            barrier()
                        if True:
                            combos = [(s, n) for s in range(4) for n in range(4)]
                            for p0 in range(0, 16, 6):
                                sub = combos[p0:p0 + 6]
                                v_gm = None
                                for kc in range(16):
                                    slot, tgt = ring_load(DT["bgw"][kc * 128:(kc + 1) * 128, :])
                                    wait("t", gq, tgt)
                                    for bi, (s4, n4) in enumerate(sub):
                                        v_gm = ins("t", lambda e, sl=slot, k=kc, s=s4, n=n4, b=bi, hf=half:
                                                   e.matmul(PSF[:, b * 512:(b + 1) * 512],
                                                            xn3[:, k * 1024 + hf * 512 + s * 128: k * 1024 + hf * 512 + (s + 1) * 128],
                                                            ring_g[:, sl * 2048 + n * 512: sl * 2048 + (n + 1) * 512],
                                                            start=(k == 0), stop=(k == 15)),
                                                   tm if bi == len(sub) - 1 else None, 1)
                                    ring_free(tm, v_gm)
                                wait("v", tm, v_gm)
                                v_ge = None
                                for bi, (s4, n4) in enumerate(sub):
                                    v_ge = ins("v", lambda e, s=s4, n=n4, b=bi, GH=GH: e.scalar_tensor_tensor(
                                        out=GH[:, s * 2048 + n * 512: s * 2048 + (n + 1) * 512],
                                        in0=PSF[:, b * 512:(b + 1) * 512],
                                        scalar=1.0,
                                        in1=BGBB[:, n * 512:(n + 1) * 512],
                                        op0=ALU.mult, op1=ALU.add), vm)
                                wait("t", vm, v_ge)
                            barrier()
                            if STAGE == 4 and half == 0:
                                dump(GH[:, 0:8192], 8192)
                                barrier()
                        OGT = SCB
                        with (
                            nc.sbuf_tensor(f"g_asb{half}", [128, 2 * 128], BF16) as ASB,
                            nc.sbuf_tensor(f"g_on{half}", [128, 512], BF16) as ON,
                            nc.sbuf_tensor(f"g_ogf{half}", [128, 2048], BF16) as OGF,
                        ):
                            psO = PSF[:, 0:1024]
                            psKV = PSF[:, 1024:2048]
                            psA = PSF[:, 2048:3072]
                            psbT = PSB
                            wvx = [0, 0, 0, 0]
                            if GLA_CUT >= 4:
                                ogfe = [0, 0]
                                for c in range(4):
                                    for n4 in range(4):
                                        if wvx[n4]:
                                            wait("t", vm, wvx[n4])
                                    v_og = None
                                    for hp in range(4):
                                        h0, h1 = 2 * hp, 2 * hp + 1
                                        for h in (h0, h1):
                                            sl = h % 2
                                            v_at = ins("t", lambda e, h=h, c=c, sl=sl, psA=psA, KN=KN, QD=QD: e.matmul(
                                                psA[:, sl * 512: sl * 512 + 128],
                                                KN[:, h * 512 + c * 128: h * 512 + (c + 1) * 128],
                                                QD[:, h * 512 + c * 128: h * 512 + (c + 1) * 128],
                                                start=True, stop=True), tm)
                                            wait("v", tm, v_at)
                                            v_as = ins("v", lambda e, h=h, sl=sl, ASB=ASB, psA=psA: e.tensor_tensor(
                                                out=ASB[:, sl * 128:(sl + 1) * 128],
                                                in0=psA[:, sl * 512: sl * 512 + 128],
                                                in1=TRI[:], op=ALU.mult), vm)
                                            wait("t", vm, v_as)
                                            ins("t", lambda e, h=h, c=c, sl=sl, psO=psO, QD=QD: e.matmul(
                                                psO[:, sl * 512: sl * 512 + 256],
                                                QD[:, h * 512 + c * 128: h * 512 + (c + 1) * 128],
                                                SB16[:, h * 256:(h + 1) * 256],
                                                start=True, stop=False))
                                            v_o = ins("t", lambda e, h=h, c=c, sl=sl, psO=psO, ASB=ASB: e.matmul(
                                                psO[:, sl * 512: sl * 512 + 256],
                                                ASB[:, sl * 128:(sl + 1) * 128],
                                                VT[:, c * 2048 + h * 256: c * 2048 + (h + 1) * 256],
                                                start=False, stop=True), tm)
                                            v_kv = ins("t", lambda e, h=h, c=c, sl=sl, psKV=psKV: e.matmul(
                                                psKV[:, sl * 512: sl * 512 + 256],
                                                KDT[:, (h * 4 + c) * 128:(h * 4 + c + 1) * 128],
                                                VT[:, c * 2048 + h * 256: c * 2048 + (h + 1) * 256],
                                                start=True, stop=True), tm)
                                        wait("v", tm, v_kv)
                                        nA, nB = 0, 24
                                        wait("s", tm, v_kv)
                                        v_sq2 = 0
                                        for hx, n in ((h0, nA), (h1, nB)):
                                            slx = hx % 2
                                            v_sq2 = ins("s", lambda e, n=n, slx=slx, psO=psO: e.activation(
                                                out=SCR[:, slx * 256: slx * 256 + 256],
                                                in_=psO[:, slx * 512: slx * 512 + 256],
                                                func=AF.Square, accum_out=tc(n + 1)), am)
                                        wait("v", am, v_sq2)
                                        for hx, n in ((h0, nA), (h1, nB)):
                                            slx = hx % 2
                                            em("v", lambda e, n=n, slx=slx, psO=psO: e.reduce_sum(
                                                out=tc(n + 0), in_=psO[:, slx * 512: slx * 512 + 256],
                                                axis=AX.X))
                                        for n in (nA, nB):
                                            em("v", lambda e, n=n: e.tensor_scalar(
                                                out=tc(n + 2), in0=tc(n + 0), scalar1=1.0 / 256,
                                                scalar2=None, op0=ALU.mult))
                                        for n in (nA, nB):
                                            em("v", lambda e, n=n: e.tensor_scalar(
                                                out=tc(n + 3), in0=tc(n + 1), scalar1=1.0 / 256,
                                                scalar2=None, op0=ALU.mult))
                                        for n in (nA, nB):
                                            em("v", lambda e, n=n: e.tensor_tensor(
                                                out=tc(n + 4), in0=tc(n + 2), in1=tc(n + 2), op=ALU.mult))
                                        for n in (nA, nB):
                                            em("v", lambda e, n=n: e.tensor_tensor(
                                                out=tc(n + 5), in0=tc(n + 3), in1=tc(n + 4), op=ALU.subtract))
                                        for n in (nA, nB):
                                            em("v", lambda e, n=n: e.tensor_scalar(
                                                out=tc(n + 5), in0=tc(n + 5), scalar1=1e-5,
                                                scalar2=None, op0=ALU.add))
                                        v_rc2 = 0
                                        for n in (nA, nB):
                                            v_rc2 = ins("v", lambda e, n=n: e.reciprocal(
                                                out=tc(n + 6), in_=tc(n + 5)), vm)
                                        wait("s", vm, v_rc2)
                                        v_rs2 = 0
                                        for n in (nA, nB):
                                            v_rs2 = ins("s", lambda e, n=n: e.activation(
                                                out=tc(n + 7), in_=tc(n + 6), func=AF.Sqrt), am)
                                        wait("v", am, v_rs2)
                                        for n in (nA, nB):
                                            em("v", lambda e, n=n: e.tensor_tensor(
                                                out=tc(18 + (n > 0)), in0=tc(n + 7), in1=tc(n + 7), op=ALU.mult))
                                        for n in (nA, nB):
                                            em("v", lambda e, n=n: e.tensor_tensor(
                                                out=tc(18 + (n > 0)), in0=tc(18 + (n > 0)), in1=tc(n + 5), op=ALU.mult))
                                        for n in (nA, nB):
                                            em("v", lambda e, n=n: e.tensor_scalar(
                                                out=tc(18 + (n > 0)), in0=tc(18 + (n > 0)),
                                                scalar1=-0.5, scalar2=1.5, op0=ALU.mult, op1=ALU.add))
                                        for n in (nA, nB):
                                            em("v", lambda e, n=n: e.tensor_tensor(
                                                out=tc(n + 6), in0=tc(n + 7), in1=tc(18 + (n > 0)), op=ALU.mult))
                                        for hx, n in ((h0, nA), (h1, nB)):
                                            slx = hx % 2
                                            em("v", lambda e, n=n, slx=slx, ON=ON, psO=psO: e.tensor_scalar(
                                                out=ON[:, slx * 256: slx * 256 + 256],
                                                in0=psO[:, slx * 512: slx * 512 + 256],
                                                scalar1=tc(n + 2), scalar2=tc(n + 6),
                                                op0=ALU.subtract, op1=ALU.mult))
                                        for hx in (h0, h1):
                                            slx = hx % 2
                                            em("v", lambda e, h=hx, hf=half, c=c, slx=slx, psKV=psKV:
                                               e.scalar_tensor_tensor(
                                                out=S[:, h * 256:(h + 1) * 256],
                                                in0=S[:, h * 256:(h + 1) * 256],
                                                scalar=DL[:, h * 8 + hf * 4 + c: h * 8 + hf * 4 + c + 1],
                                                in1=psKV[:, slx * 512: slx * 512 + 256],
                                                op0=ALU.mult, op1=ALU.add))
                                        v_sb2 = 0
                                        for hx in (h0, h1):
                                            v_sb2 = ins("v", lambda e, h=hx: e.tensor_copy(
                                                out=SB16[:, h * 256:(h + 1) * 256],
                                                in_=S[:, h * 256:(h + 1) * 256]), vm)
                                        wait("t", vm, v_sb2)
                                        wait("s", vm, v_sb2)
                                        v_gs = ins("s", lambda e, c=c, h0=h0, GH=GH: e.activation(
                                            out=SCR[:, 1024:1536],
                                            in_=GH[:, c * 2048 + h0 * 256: c * 2048 + h0 * 256 + 512],
                                            func=AF.Silu), am)
                                        wait("v", am, v_gs)
                                        v_og = ins("v", lambda e, h0=h0, OGT=OGT, ON=ON: e.tensor_tensor(
                                            out=OGT[:, h0 * 256: h0 * 256 + 512],
                                            in0=SCR[:, 1024:1536], in1=ON[:, 0:512],
                                            op=ALU.mult), vm)
                                    if GLA_CUT >= 7:
                                        wait("t", vm, v_og)
                                        for dcp in range(4):
                                            hfp = (dcp % 2) * 1024
                                            if ogfe[dcp % 2]:
                                                wait("t", vm, ogfe[dcp % 2])
                                            v_tp = None
                                            for j in range(4):
                                                dcc = dcp * 4 + j
                                                v_tp = ins("t", lambda e, d=dcc, hfp=hfp, j=j, psbT=psbT, OGT=OGT: e.transpose(
                                                    psbT[:, hfp + j * 128: hfp + (j + 1) * 128],
                                                    OGT[:, d * 128:(d + 1) * 128], IDB[:]),
                                                    tm if j == 3 else None)
                                            wait("v", tm, v_tp)
                                            ogfe[dcp % 2] = ins("v", lambda e, dcp=dcp, hfp=hfp, OGF=OGF, psbT=psbT: e.tensor_copy(
                                                out=OGF[:, dcp * 512:(dcp + 1) * 512],
                                                in_=psbT[:, hfp: hfp + 512]), vm)
                                        wait("t", vm, max(ogfe))
                                        v_wo = None
                                        for kc in range(16):
                                            slot, tgt = ring_load(DT["bo"][kc * 128:(kc + 1) * 128, :])
                                            wait("t", gq, tgt)
                                            if kc == 0:
                                                for n4 in range(4):
                                                    if wvx[n4]:
                                                        wait("t", vm, wvx[n4])
                                            for n4 in range(4):
                                                pso = psO if n4 < 2 else psKV
                                                v_wo = ins("t", lambda e, sl=slot, k=kc, n=n4, p=pso, OGF=OGF:
                                                           e.matmul(p[:, (n % 2) * 512:(n % 2) * 512 + 512],
                                                                    OGF[:, k * 128:(k + 1) * 128],
                                                                    ring_g[:, sl * 2048 + n * 512: sl * 2048 + (n + 1) * 512],
                                                                    start=(k == 0), stop=(k == 15)),
                                                           tm if n4 == 3 else None, 1)
                                            ring_free(tm, v_wo)
                                        wait("v", tm, v_wo)
                                        ti = half * 4 + c
                                        for n4 in range(4):
                                            pso = psO if n4 < 2 else psKV
                                            wvx[n4] = ins("v", lambda e, n=n4, ti=ti, p=pso:
                                                          e.scalar_tensor_tensor(
                                                out=X[:, ti * 2048 + n * 512: ti * 2048 + (n + 1) * 512],
                                                in0=p[:, (n % 2) * 512:(n % 2) * 512 + 512], scalar=1.0,
                                                in1=X[:, ti * 2048 + n * 512: ti * 2048 + (n + 1) * 512],
                                                op0=ALU.mult, op1=ALU.add), vm)

                        wait("t", vm, max(wvx))
                        barrier()
                        if STAGE == 4 and half == 0:
                            dump(X[:, 0:8192], 8192)
                            barrier()

        gst.close()
        barrier()

    if STAGE >= 2:
        sub_attn_a()
    if STAGE >= 3:
        sub_moe(1)
    if STAGE >= 4:
        sub_gla()
    if STAGE >= 5:
        sub_moe(2)

    v_fin = None
    for i in range(NT):
        v_fin = ins("s", lambda e, i=i: e.dma_start(
            out=y_d[i * 128:(i + 1) * 128, :], in_=X[:, i * 2048:(i + 1) * 2048]), aq, 16)
    wait("s", aq, v_fin)

    with nc.Block() as block:
        @block.tensor
        def _(e):
            for f in Q["t"]:
                f(e)

        @block.vector
        def _(e):
            for f in Q["v"]:
                f(e)

        @block.scalar
        def _(e):
            for f in Q["s"]:
                f(e)

        @block.sync
        def _(e):
            for f in Q["y"]:
                f(e)

        @block.gpsimd
        def _(e):
            for f in Q["g"]:
                f(e)

    stack.close()
    return nc


_NC_CACHE = [None]
_LAST_RES = [None]


def kernel(**inputs):
    in_maps = _host_prep(inputs)
    if _NC_CACHE[0] is None:
        _NC_CACHE[0] = build_program()
    nc = _NC_CACHE[0]
    res = run_bass_kernel_spmd(nc, in_maps, core_ids=list(range(NCORE)))
    _LAST_RES[0] = res
    ys = [np.asarray(res.results[c]["y"]) for c in range(NCORE)]
    return np.concatenate(ys, axis=0).reshape(B, S, D)
